# version 21
# baseline (speedup 1.0000x reference)
import numpy as np
import ml_dtypes
from contextlib import ExitStack

import concourse.bass as bass
import concourse.mybir as mybir
from concourse.bass_utils import run_bass_kernel_spmd

F32 = mybir.dt.float32
BF16 = mybir.dt.bfloat16
AF = mybir.ActivationFunctionType
ALU = mybir.AluOpType
BFNP = ml_dtypes.bfloat16

D = 1024
S = 4096
NT = 8
NC = 8
H = 16
HD = 64
FF = 32
KR = 3
VR = 192
NJ = 32
EPS = 1e-5

_PHASES = 99


class Ctr:
    __slots__ = ("h", "n")

    def __init__(self, h):
        self.h = h
        self.n = 0

    def inc(self, instr, k=1):
        instr.then_inc(self.h, k)
        self.n += k
        return self.n


class MK:
    def __init__(self, nc, stack):
        self.nc = nc
        self.stack = stack
        E = stack.enter_context
        self.T, self.S, self.V, self.Q, self.G = nc.tensor, nc.scalar, nc.vector, nc.sync, nc.gpsimd

        self.dxT = nc.dram_tensor("xT", [D, S], F32, kind="ExternalInput")
        self.d_out = nc.dram_tensor("outT", [D, S], F32, kind="ExternalOutput")
        self.scr1 = nc.dram_tensor("scr1", [D, S], F32, kind="Internal")
        self.scr2 = nc.dram_tensor("scr2", [D, S], F32, kind="Internal")
        self.scr3 = nc.dram_tensor("scr3", [D, S], F32, kind="Internal")
        self.scr_h = nc.dram_tensor("scr_h", [4096, S], BF16, kind="Internal")
        self.yscr = nc.dram_tensor("yscr", [D, S], BF16, kind="Internal")
        self.drec = nc.dram_tensor("drec", [H, S], BF16, kind="Internal")
        self.d_ksum = nc.dram_tensor("d_ksum", [D], BF16, kind="Internal")
        di = {}
        for name, shape in [
            ("WqaT", [D + 1, D]), ("bqa", [128, NT]), ("WkaT", [D + 1, D]),
            ("WvaT", [D + 1, D]), ("WoaT", [D, D]), ("boa", [1, D]),
            ("w1", [D, 4096]), ("b1c", [128, NJ]), ("w2", [4096, D]),
            ("b2c", [128, NT]), ("Wqb_perm", [D, D]), ("Wkb", [D, VR]),
            ("Wvb", [D, VR]), ("KDpos", [VR, D]), ("KDneg", [VR, D]),
            ("Vdec", [VR, D]), ("Wproj", [D, D]),
        ]:
            di[name] = nc.dram_tensor(name, shape, F32, kind="ExternalInput")
        for name, shape in [("COSR", [128, S]), ("SINR", [128, S]),
                            ("LMAT", [128, 128]), ("IDENT", [128, 128]),
                            ("SEL2", [2, 128]), ("TRIM", [128, 128])]:
            di[name] = nc.dram_tensor(name, shape, BF16, kind="ExternalInput")
        self.di = di

        self.xnt = [E(nc.sbuf_tensor(f"xnt{t}", [128, S], BF16)) for t in range(NT)]
        self.lmat = E(nc.sbuf_tensor("s_lmat", [128, 128], BF16))
        self.ident = E(nc.sbuf_tensor("s_ident", [128, 128], BF16))
        self.ones_r16 = E(nc.sbuf_tensor("ones_r16", [1, 128], BF16))
        self.ones_c16 = E(nc.sbuf_tensor("ones_c16", [128, 1], BF16))
        self.ones_rf = E(nc.sbuf_tensor("ones_rf", [1, 128], F32))
        self.ones512 = E(nc.sbuf_tensor("ones512", [1, 512], BF16))
        self.bqa_s = E(nc.sbuf_tensor("bqa_s", [128, NT], F32))
        self.b1_s = E(nc.sbuf_tensor("b1_s", [128, NJ], F32))
        self.b2_s = E(nc.sbuf_tensor("b2_s", [128, NT], F32))
        self.boa16 = E(nc.sbuf_tensor("s_boa16", [1, D], BF16))
        self.r_mean = E(nc.sbuf_tensor("r_mean", [1, 512], F32))
        self.r_var = E(nc.sbuf_tensor("r_var", [1, 512], F32))
        self.r_rstd = E(nc.sbuf_tensor("r_rstd", [1, 512], F32))
        self.tln = E(nc.sbuf_tensor("s_tln", [128, 512], F32))

        self.st = Ctr(E(nc.semaphore("st")))
        self.ss = Ctr(E(nc.semaphore("ss")))
        self.sv = Ctr(E(nc.semaphore("sv")))
        self.sq = Ctr(E(nc.semaphore("sq")))
        self.sg = Ctr(E(nc.semaphore("sg")))
        self.sr = Ctr(E(nc.semaphore("sr")))
        self.sh = [Ctr(E(nc.semaphore("sh0"))), Ctr(E(nc.semaphore("sh1")))]
        self.sx = Ctr(E(nc.semaphore("sx")))
        self.shh = [Ctr(E(nc.semaphore(f"shh{i}"))) for i in range(4)]
        self.swt = [Ctr(E(nc.semaphore(f"swt{i}"))) for i in range(33)]
        self._wt_waited = set()
        self._wt_epoch = 0

        self.lq(self.lmat[:, :], di["LMAT"][:, :])
        self.lq(self.ident[:, :], di["IDENT"][:, :])
        self.lq(self.bqa_s[:, :], di["bqa"][:, :])
        self.lq(self.b1_s[:, :], di["b1c"][:, :])
        self.lq(self.b2_s[:, :], di["b2c"][:, :])
        self.lg(self.boa16[:, :], di["boa"][:, :])
        self.V.memset(self.ones_r16[:, :], 1.0)
        self.V.memset(self.ones512[:, :], 1.0)
        self.V.memset(self.ones_c16[:, :], 1.0)
        self.V.memset(self.tln[:, :], 0.0)
        self.sv.inc(self.V.memset(self.ones_rf[:, :], 1.0))
        self.init_v = self.sv.n
        self.V.drain()

    def w(self, eng, c, v):
        assert 0 <= v <= c.n, f"wait target {v} > emitted {c.n}"
        if v > 0:
            eng.wait_ge(c.h, v)

    def _ninstr(self):
        return sum(len(b.instructions) for b in self.nc.m.functions[0].basic_blocks) \
            if False else sum(1 for _ in self.nc.all_instructions())

    def lq(self, dst, src):
        n0 = self._ninstr()
        ins = self.Q.dma_start(dst, src)
        k = self._ninstr() - n0
        ins.then_inc(self.sq.h, 16)
        self.sq.n += 16 * k
        return self.sq.n

    def lg(self, dst, src):
        n0 = self._ninstr()
        ins = self.G.dma_start(dst, src)
        k = self._ninstr() - n0
        ins.then_inc(self.sg.h, 16)
        self.sg.n += 16 * k
        return self.sg.n

    def lg_w(self, idx, dst, src):
        c = self.swt[idx]
        n0 = self._ninstr()
        ins = self.G.dma_start(dst, src)
        k = self._ninstr() - n0
        ins.then_inc(c.h, 16)
        c.n += 16 * k
        return c.n

    def wt(self, idx):
        key = (self._wt_epoch, idx)
        if key not in self._wt_waited:
            self.w(self.T, self.swt[idx], self.swt[idx].n)
            self._wt_waited.add(key)

    def new_wt_epoch(self):
        self._wt_epoch += 1

    def lq_on(self, ctr, dst, src):
        n0 = self._ninstr()
        ins = self.Q.dma_start(dst, src)
        k = self._ninstr() - n0
        ins.then_inc(ctr.h, 16)
        ctr.n += 16 * k
        return ctr.n

    def barrier(self):
        T, Sc, V, Q, G = self.T, self.S, self.V, self.Q, self.G
        vv = self.sv.inc(V.tensor_copy(self.tln[0:1, 0:4], self.tln[0:1, 0:4]))
        sw = self.ss.inc(Sc.copy(self.tln[0:1, 4:8], self.tln[0:1, 4:8]))
        tv = self.st.n
        for eng in (T, Sc, V, Q, G):
            self.w(eng, self.st, tv)
            self.w(eng, self.sv, vv)
            self.w(eng, self.ss, sw)
            self.w(eng, self.sq, self.sq.n)
            self.w(eng, self.sg, self.sg.n)
            self.w(eng, self.sr, self.sr.n)
            self.w(eng, self.sx, self.sx.n)
            self.w(eng, self.sh[0], self.sh[0].n)
            self.w(eng, self.sh[1], self.sh[1].n)
            for _c in self.shh:
                self.w(eng, _c, _c.n)

    def load_chunk_f32(self, dst, dram, c0, width=512, ctr=None):
        return self.lq_on(
            self.sx if ctr is None else ctr,
            dst[:, :].rearrange("p (t s) -> p t s", s=width),
            dram[:, c0:c0 + width].rearrange("(t p) s -> p t s", p=128))

    def emit_ln(self, src, xb, xsq, dst_tiles, dc0, pstats, pbm, pbr, rows=None,
                xb_free=0):
        T, Sc, V = self.T, self.S, self.V
        st, ss, sv = self.st, self.ss, self.sv
        r_mean, r_var, r_rstd = rows if rows is not None else (
            self.r_mean, self.r_var, self.r_rstd)
        w = 512
        V.drain()
        self.w(Sc, sv, xb_free)
        ss.inc(Sc.activation(xb[:, :], src[:, :], AF.Copy))
        e_sq = ss.inc(Sc.activation(xsq[:, :], src[:, :], AF.Square))
        self.w(T, ss, e_sq)
        self.w(T, sv, self.init_v)
        for t in range(NT):
            T.matmul(pstats[0:1, :], self.ones_c16[:, :], xb[:, t * w:(t + 1) * w],
                     start=(t == 0), stop=(t == NT - 1))
        for t in range(NT):
            ins = T.matmul(pstats[32:33, :], self.ones_c16[:, :], xsq[:, t * w:(t + 1) * w],
                           start=(t == 0), stop=(t == NT - 1))
        e_stats = st.inc(ins)
        self.w(V, st, e_stats)
        V.tensor_single_scalar(r_mean[:, :], pstats[0:1, :], 1.0 / D, ALU.mult)
        V.tensor_scalar(r_var[:, :], pstats[32:33, :], 1.0 / D, EPS,
                        op0=ALU.mult, op1=ALU.add)
        V.drain()
        V.tensor_tensor(r_rstd[:, :], r_mean[:, :], r_mean[:, :], ALU.mult)
        V.drain()
        e_rows = sv.inc(V.tensor_tensor(r_var[:, :], r_var[:, :],
                                        r_rstd[:, :], ALU.subtract))
        self.w(Sc, sv, e_rows)
        Sc.activation(r_rstd[:, :], r_var[:, :], AF.Ln)
        Sc.drain()
        e_rstd = ss.inc(Sc.activation(r_rstd[:, :], r_rstd[:, :],
                                      AF.Exp, scale=-0.5))
        self.w(T, sv, e_rows)
        self.w(T, ss, e_rstd)
        T.matmul(pbm[:, :], self.ones_rf[:, :], r_mean[:, :], start=True, stop=True)
        e_bc = st.inc(T.matmul(pbr[:, :], self.ones_rf[:, :], r_rstd[:, :],
                               start=True, stop=True))
        self.w(V, st, e_bc)
        for t in range(NT):
            V.tensor_tensor(xb[:, t * w:(t + 1) * w], src[:, t * w:(t + 1) * w],
                            pbm[:, :], ALU.subtract)
        V.drain()
        for t in range(NT):
            ins = V.tensor_tensor(dst_tiles[t][:, dc0:dc0 + w], xb[:, t * w:(t + 1) * w],
                                  pbr[:, :], ALU.mult)
        return sv.inc(ins)

    def phase0(self):
        nc, T, Sc, V, Q = self.nc, self.T, self.S, self.V, self.Q
        with ExitStack() as p:
            E = p.enter_context
            xf = [E(nc.sbuf_tensor(f"p0_xf{i}", [128, NT * 512], F32)) for i in range(2)]
            xb = [E(nc.sbuf_tensor(f"p0_xb{i}", [128, NT * 512], BF16)) for i in range(2)]
            xsq = [E(nc.sbuf_tensor(f"p0_xsq{i}", [128, NT * 512], BF16)) for i in range(2)]
            rows = [(E(nc.sbuf_tensor(f"p0_rm{i}", [1, 512], F32)),
                     E(nc.sbuf_tensor(f"p0_rv{i}", [1, 512], F32)),
                     E(nc.sbuf_tensor(f"p0_rs{i}", [1, 512], F32))) for i in range(2)]
            pstats = [E(nc.psum_tensor(f"p0_pstats{i}", [33, 512], F32)) for i in range(2)]
            pbm = [E(nc.psum_tensor(f"p0_pbm{i}", [128, 512], F32)) for i in range(2)]
            pbr = [E(nc.psum_tensor(f"p0_pbr{i}", [128, 512], F32)) for i in range(2)]
            ln_done = [0, 0]
            for ch in range(8):
                b = ch % 2
                c0 = ch * 512
                self.w(Q, self.sv, ln_done[b])
                e_x = self.load_chunk_f32(xf[b], self.dxT, c0, ctr=self.sh[b])
                self.w(V, self.sh[b], e_x)
                self.w(Sc, self.sh[b], e_x)
                ln_done[b] = self.emit_ln(xf[b], xb[b], xsq[b], self.xnt, c0,
                                          pstats[b], pbm[b], pbr[b], rows=rows[b],
                                          xb_free=ln_done[b])
        self.barrier()

    def attn_a(self):
        nc, T, Sc, V, Q = self.nc, self.T, self.S, self.V, self.Q
        st, ss, sv, sq, sg = self.st, self.ss, self.sv, self.sq, self.sg
        di = self.di
        with ExitStack() as outer:
            EO = outer.enter_context
            KV16 = [EO(nc.sbuf_tensor(f"KV16_{t}", [128, D], BF16)) for t in range(NT)]
            ksumT = EO(nc.sbuf_tensor("ksumT", [128, NT], BF16))

            with ExitStack() as pa:
                E = pa.enter_context
                wka = [E(nc.sbuf_tensor(f"wka{t}", [128, D], BF16)) for t in range(NT)]
                wka_b = E(nc.sbuf_tensor("wka_b", [1, D], BF16))
                wva = [E(nc.sbuf_tensor(f"wva{t}", [128, D], BF16)) for t in range(NT)]
                wva_b = E(nc.sbuf_tensor("wva_b", [1, D], BF16))
                kbuf = [E(nc.sbuf_tensor(f"kbuf{t}", [128, D], BF16)) for t in range(NT)]
                vbuf = [E(nc.sbuf_tensor(f"vbuf{t}", [128, D], BF16)) for t in range(NT)]
                KV32 = [E(nc.sbuf_tensor(f"KV32_{t}", [128, D], F32)) for t in range(NT)]
                tm = [E(nc.sbuf_tensor(f"tm{i}", [128, 512], F32)) for i in range(2)]
                r_ks = E(nc.sbuf_tensor("r_ks", [1, D], F32))
                r_ks16 = E(nc.sbuf_tensor("r_ks16", [1, D], BF16))
                ppk = [E(nc.psum_tensor(f"ppk{i}", [128, 512], F32)) for i in range(2)]
                ppv = [E(nc.psum_tensor(f"ppv{i}", [128, 512], F32)) for i in range(2)]
                pkv = [E(nc.psum_tensor(f"pkv{i}", [128, 512], F32)) for i in range(2)]
                pks = [E(nc.psum_tensor(f"pks{i}", [1, 512], F32)) for i in range(2)]

                self.new_wt_epoch()
                for t in range(NT):
                    self.lg_w(t, wka[t][:, :], di["WkaT"][t * 128:(t + 1) * 128, :])
                    self.lg_w(8 + t, wva[t][:, :], di["WvaT"][t * 128:(t + 1) * 128, :])
                self.lg_w(16, wka_b[:, :], di["WkaT"][D:D + 1, :])
                self.lg_w(17, wva_b[:, :], di["WvaT"][D:D + 1, :])

                k_free = [0, 0]
                v_free = [0, 0]
                kv_free = [0, 0]
                kbuf_cons = 0
                tm_free = [0, 0]
                te_free = [0, 0]
                ei = 0
                for ch in range(4):
                    for tt in range(NT):
                        gt = ch * 8 + tt
                        sl = slice(gt * 128, gt * 128 + 128)
                        for half in range(2):
                            hs = slice(half * 512, half * 512 + 512)
                            self.w(T, sv, k_free[half])
                            for kt in range(NT):
                                self.wt(kt)
                                T.matmul(ppk[half][:, :], self.xnt[kt][:, sl], wka[kt][:, hs],
                                         start=(kt == 0), stop=False)
                            self.wt(16)
                            e_k = st.inc(T.matmul(ppk[half][:, :], self.ones_r16[:, 0:128],
                                                  wka_b[:, hs], start=False, stop=True))
                            self.w(T, sv, v_free[half])
                            for kt in range(NT):
                                self.wt(8 + kt)
                                T.matmul(ppv[half][:, :], self.xnt[kt][:, sl], wva[kt][:, hs],
                                         start=(kt == 0), stop=False)
                            self.wt(17)
                            e_v = st.inc(T.matmul(ppv[half][:, :], self.ones_r16[:, 0:128],
                                                  wva_b[:, hs], start=False, stop=True))
                            self.w(V, st, e_k)
                            self.w(V, ss, tm_free[ei])
                            self.w(V, st, kbuf_cons)
                            self.w(V, sv, te_free[ei])
                            V.drain()
                            e_m = sv.inc(V.tensor_single_scalar(tm[ei][:, :], ppk[half][:, :],
                                                                0.0, ALU.min))
                            self.w(Sc, sv, e_m)
                            tm_free[ei] = ss.inc(Sc.activation(tm[ei][:, :], tm[ei][:, :], AF.Exp))
                            self.w(V, ss, tm_free[ei])
                            k_free[half] = sv.inc(V.scalar_tensor_tensor(
                                kbuf[tt][:, hs], ppk[half][:, :], 0.0, tm[ei][:, :],
                                op0=ALU.max, op1=ALU.add))
                            te_free[ei] = k_free[half]
                            ei ^= 1
                            self.w(V, st, e_v)
                            self.w(V, st, kbuf_cons)
                            v_free[half] = sv.inc(V.tensor_copy(vbuf[tt][:, hs], ppv[half][:, :]))
                    e_tiles = sv.n
                    self.w(T, sv, e_tiles)
                    for dt in range(NT):
                        for eh in range(2):
                            b = (dt * 2 + eh) % 2
                            self.w(T, sv, kv_free[b])
                            for tt in range(NT):
                                ins = T.matmul(pkv[b][:, :], kbuf[tt][:, dt * 128:dt * 128 + 128],
                                               vbuf[tt][:, eh * 512:eh * 512 + 512],
                                               start=(tt == 0), stop=(tt == NT - 1))
                            e_kv = st.inc(ins)
                            self.w(V, st, e_kv)
                            if ch == 0:
                                kv_free[b] = sv.inc(V.tensor_copy(
                                    KV32[dt][:, eh * 512:eh * 512 + 512], pkv[b][:, :]))
                            else:
                                V.drain()
                                kv_free[b] = sv.inc(V.tensor_tensor(
                                    KV32[dt][:, eh * 512:eh * 512 + 512],
                                    KV32[dt][:, eh * 512:eh * 512 + 512],
                                    pkv[b][:, :], ALU.add))
                    for tt in range(NT):
                        for eh in range(2):
                            ins = T.matmul(pks[eh][:, :], self.ones_c16[:, :],
                                           kbuf[tt][:, eh * 512:eh * 512 + 512],
                                           start=(ch == 0 and tt == 0),
                                           stop=(ch == 3 and tt == NT - 1))
                    kbuf_cons = st.inc(ins)

                self.w(V, st, st.n)
                V.drain()
                V.tensor_single_scalar(r_ks[:, 0:512], pks[0][:, :], 1e-6, ALU.add)
                V.tensor_single_scalar(r_ks[:, 512:1024], pks[1][:, :], 1e-6, ALU.add)
                V.drain()
                e_ksf = sv.inc(V.tensor_copy(r_ks16[:, :], r_ks[:, :]))
                self.w(Q, sv, e_ksf)
                e_st = self.lq(self.d_ksum[:].rearrange("(o d) -> o d", o=1), r_ks16[:, :])
                self.w(Q, sq, e_st)
                for t in range(NT):
                    self.lq(ksumT[:, t:t + 1],
                            self.d_ksum[t * 128:(t + 1) * 128].rearrange("(p o) -> p o", o=1))
                V.drain()
                for dt in range(NT):
                    sv.inc(V.tensor_copy(KV16[dt][:, :], KV32[dt][:, :]))
                self.barrier()

            with ExitStack() as pb:
                E = pb.enter_context
                wqa = [E(nc.sbuf_tensor(f"wqa{t}", [128, D], BF16)) for t in range(NT)]
                wqa_b = E(nc.sbuf_tensor("wqa_b", [1, D], BF16))
                woa = [E(nc.sbuf_tensor(f"woa{t}", [128, D], BF16)) for t in range(NT)]
                qch = [E(nc.sbuf_tensor(f"qch{t}", [128, 512], BF16)) for t in range(NT)]
                cch = [E(nc.sbuf_tensor(f"cch{t}", [128, 512], BF16)) for t in range(NT)]
                xf = E(nc.sbuf_tensor("p1_xf", [128, NT * 512], F32))
                x1f = E(nc.sbuf_tensor("p1_x1f", [128, NT * 512], F32))
                xb = E(nc.sbuf_tensor("p1_xb", [128, NT * 512], BF16))
                xsq = E(nc.sbuf_tensor("p1_xsq", [128, NT * 512], BF16))
                tm = [E(nc.sbuf_tensor(f"tm1_{i}", [128, 512], F32)) for i in range(2)]
                zbs = E(nc.sbuf_tensor("zbs", [128, 512], F32))
                r_z = self.r_mean
                r_rz = self.r_var
                r_z16 = E(nc.sbuf_tensor("r_z16", [1, 512], BF16))
                ppa = [E(nc.psum_tensor(f"ppa{i}", [128, 512], F32)) for i in range(2)]
                ppb = [E(nc.psum_tensor(f"ppb{i}", [128, 512], F32)) for i in range(2)]
                ppt = [E(nc.psum_tensor(f"ppt{i}", [128, 512], F32)) for i in range(2)]
                pzb = E(nc.psum_tensor("pzb", [128, 512], F32))
                pstats = E(nc.psum_tensor("p1_pstats", [33, 512], F32))

                self.new_wt_epoch()
                for t in range(NT):
                    self.lg_w(t, wqa[t][:, :], di["WqaT"][t * 128:(t + 1) * 128, :])
                self.lg_w(16, wqa_b[:, :], di["WqaT"][D:D + 1, :])
                for t in range(NT):
                    self.lg_w(8 + t, woa[t][:, :], di["WoaT"][t * 128:(t + 1) * 128, :])
                self.w(T, sq, sq.n)

                ei = 0
                qfree = [0, 0]
                cfree = [0, 0]
                ep_evt = [0] * NT
                for ch in range(8):
                    c0 = ch * 512
                    cols = slice(c0, c0 + 512)
                    self.w(Q, sv, sv.n if ch > 0 else 0)
                    e_x = self.load_chunk_f32(xf, self.dxT, c0)
                    for ot in range(NT):
                        b = ot % 2
                        self.w(T, sv, qfree[b])
                        for kt in range(NT):
                            self.wt(kt)
                            T.matmul(ppa[b][:, :], wqa[kt][:, ot * 128:ot * 128 + 128],
                                     self.xnt[kt][:, cols], start=(kt == 0), stop=False)
                        self.wt(16)
                        e_q = st.inc(T.matmul(
                            ppa[b][:, :], wqa_b[:, ot * 128:ot * 128 + 128],
                            self.ones512[:, :], start=False, stop=True))
                        self.w(V, st, e_q)
                        V.drain()
                        e_m = sv.inc(V.tensor_single_scalar(tm[ei][:, :], ppa[b][:, :],
                                                            0.0, ALU.min))
                        self.w(Sc, sv, e_m)
                        e_e = ss.inc(Sc.activation(tm[ei][:, :], tm[ei][:, :], AF.Exp))
                        self.w(V, ss, e_e)
                        qfree[b] = sv.inc(
                            V.scalar_tensor_tensor(qch[ot][:, :], ppa[b][:, :], 0.0,
                                                   tm[ei][:, :], op0=ALU.max, op1=ALU.add))
                        ei ^= 1
                    e_qdone = sv.n
                    self.w(T, sv, e_qdone)
                    for dt in range(NT):
                        ins = T.matmul(pstats[0:1, :], ksumT[:, dt:dt + 1], qch[dt][:, :],
                                       start=(dt == 0), stop=(dt == NT - 1))
                    e_zmm = st.inc(ins)
                    self.w(V, st, e_zmm)
                    V.drain()
                    V.tensor_copy(r_z[:, :], pstats[0:1, :])
                    V.drain()
                    V.reciprocal(r_rz[:, :], r_z[:, :])
                    e_z = sv.inc(V.tensor_copy(r_z16[:, :], r_z[:, :]))
                    for et in range(NT):
                        b = et % 2
                        self.w(T, sv, cfree[b])
                        for dt in range(NT):
                            ins = T.matmul(ppb[b][:, :], KV16[dt][:, et * 128:et * 128 + 128],
                                           qch[dt][:, :], start=(dt == 0), stop=(dt == NT - 1))
                        e_c = st.inc(ins)
                        self.w(V, st, e_c)
                        cfree[b] = sv.inc(V.tensor_copy(cch[et][:, :], ppb[b][:, :]))
                    e_cdone = sv.n
                    self.w(T, sv, e_cdone)
                    self.w(T, sv, e_z)
                    e_zb = st.inc(T.matmul(pzb[:, :], self.ones_rf[:, :], r_rz[:, :],
                                           start=True, stop=True))
                    self.w(V, st, e_zb)
                    e_zbs = sv.inc(V.tensor_copy(zbs[:, :], pzb[:, :]))
                    e_t1 = [0] * NT
                    for ot in range(NT):
                        b = ot % 2
                        self.w(T, sv, ep_evt[ot - 2] if ot >= 2 else ep_evt[NT - 2 + ot])
                        for et in range(NT):
                            self.wt(8 + et)
                            T.matmul(ppt[b][:, :], woa[et][:, ot * 128:ot * 128 + 128],
                                     cch[et][:, :], start=(et == 0), stop=False)
                        e_t1[ot] = st.inc(T.matmul(ppt[b][:, :], self.boa16[:, ot * 128:ot * 128 + 128],
                                                   r_z16[:, :], start=False, stop=True))
                        self.w(V, st, e_t1[ot])
                        osl = slice(ot * 512, ot * 512 + 512)
                        if ot == 0:
                            self.w(V, self.sr, self.sr.n)
                            V.drain()
                        ep_evt[ot] = sv.inc(
                            V.tensor_tensor(x1f[:, osl], ppt[b][:, :], zbs[:, :], ALU.mult))
                    self.w(V, self.sx, self.sx.n)
                    V.drain()
                    for ot in range(NT):
                        osl = slice(ot * 512, ot * 512 + 512)
                        sv.inc(V.tensor_tensor(x1f[:, osl], x1f[:, osl], xf[:, osl], ALU.add))
                    e_ep = sv.n
                    self.w(Q, sv, e_ep)
                    self.lq_on(self.sr,
                               self.scr1[:, cols].rearrange("(t p) s -> p t s", p=128),
                               x1f[:, :].rearrange("p (t s) -> p t s", s=512))
                    self.w(V, sv, e_ep)
                    self.w(Sc, sv, e_ep)
                    e_ln = self.emit_ln(x1f, xb, xsq, self.xnt, c0, pstats,
                                        ppt[0], ppt[1], xb_free=ep_evt[NT - 2])
                    ep_evt[NT - 2] = e_ln
                    ep_evt[NT - 1] = e_ln
                self.barrier()

    def ffn(self, src_dram, dst_dram, do_ln, tag):
        nc, T, Sc, V, Q = self.nc, self.T, self.S, self.V, self.Q
        st, ss, sv, sq, sg = self.st, self.ss, self.sv, self.sq, self.sg
        di = self.di
        with ExitStack() as p1:
            E = p1.enter_context
            w1s = [E(nc.sbuf_tensor(f"{tag}w1_{t}", [128, 4096], BF16)) for t in range(NT)]
            hstage = [E(nc.sbuf_tensor(f"{tag}hs{i}", [128, 512], BF16)) for i in range(2)]
            pph = [E(nc.psum_tensor(f"{tag}pph{i}", [128, 512], F32)) for i in range(2)]
            self.new_wt_epoch()
            for t in range(NT):
                self.lg_w(t, w1s[t][:, :], di["w1"][t * 128:(t + 1) * 128, :])
            h_free = [0, 0]
            for ch in range(8):
                cols = slice(ch * 512, ch * 512 + 512)
                for jt in range(NJ):
                    b = jt % 2
                    self.w(T, ss, h_free[b])
                    for kt in range(NT):
                        self.wt(kt)
                        ins = T.matmul(pph[b][:, :], w1s[kt][:, jt * 128:jt * 128 + 128],
                                       self.xnt[kt][:, cols], start=(kt == 0), stop=(kt == NT - 1))
                    e_h = st.inc(ins)
                    self.w(Sc, st, e_h)
                    self.w(Sc, self.sh[b], self.sh[b].n)
                    h_free[b] = ss.inc(Sc.activation(hstage[b][:, :], pph[b][:, :], AF.Gelu,
                                                     bias=self.b1_s[:, jt:jt + 1]))
                    self.w(Q, ss, h_free[b])
                    self.lq_on(self.sh[b], self.scr_h[jt * 128:jt * 128 + 128, cols],
                               hstage[b][:, :])
        self.barrier()
        with ExitStack() as p2:
            E = p2.enter_context
            w2s = [E(nc.sbuf_tensor(f"{tag}w2_{j}", [128, D], BF16)) for j in range(NJ)]
            hb = [E(nc.sbuf_tensor(f"{tag}hb{i}", [128, 4 * 512], BF16)) for i in range(4)]
            xf = E(nc.sbuf_tensor(f"{tag}xf", [128, NT * 512], F32))
            x2f = E(nc.sbuf_tensor(f"{tag}x2f", [128, NT * 512], F32))
            xb = E(nc.sbuf_tensor(f"{tag}xb", [128, NT * 512], BF16))
            xsq = E(nc.sbuf_tensor(f"{tag}xsq", [128, NT * 512], BF16))
            pyo = [E(nc.psum_tensor(f"{tag}pyo{o}", [128, 512], F32)) for o in range(NT)]
            self.new_wt_epoch()
            for j in range(NJ):
                self.lg_w(j, w2s[j][:, :], di["w2"][j * 128:(j + 1) * 128, :])
            hb_free = [0, 0, 0, 0]
            pyo_free = [0] * NT
            e_ep_prev = 0
            for ch in range(8):
                cols = slice(ch * 512, ch * 512 + 512)
                e_x = None

                def ygrp(g, gi, ots):
                    for ot in ots:
                        if g == 0:
                            self.w(T, sv, pyo_free[ot])
                        for jj in range(4):
                            jt = g * 4 + jj
                            self.wt(jt)
                            ins = T.matmul(pyo[ot][:, :],
                                           w2s[jt][:, ot * 128:ot * 128 + 128],
                                           hb[gi][:, jj * 512:jj * 512 + 512],
                                           start=(jt == 0), stop=(jt == NJ - 1))
                    return ins

                for g in range(8):
                    gi = (ch * 8 + g) % 4
                    self.w(Q, st, hb_free[gi])
                    e_hb = self.lq_on(
                        self.shh[gi],
                        hb[gi][:, :].rearrange("p (t s) -> p t s", s=512),
                        self.scr_h[g * 512:(g + 1) * 512, cols].rearrange(
                            "(t p) s -> p t s", p=128))
                    if g == 0:
                        self.w(Q, sv, e_ep_prev)
                        e_x = self.load_chunk_f32(xf, src_dram, ch * 512)
                    self.w(T, self.shh[gi], e_hb)
                    ins = ygrp(g, gi, (3, 4, 5, 6, 7))
                    if g >= 2:
                        g2 = g - 2
                        ins = ygrp(g2, (ch * 8 + g2) % 4, (0, 1, 2))
                        hb_free[(ch * 8 + g2) % 4] = st.inc(ins)
                for g2 in (6, 7):
                    ins = ygrp(g2, (ch * 8 + g2) % 4, (0, 1, 2))
                    hb_free[(ch * 8 + g2) % 4] = st.inc(ins)
                e_y = st.n
                self.w(V, st, e_y)
                self.w(V, self.sx, self.sx.n)
                self.w(V, self.sr, self.sr.n)
                V.drain()
                for ot in range(NT):
                    osl = slice(ot * 512, ot * 512 + 512)
                    pyo_free[ot] = sv.inc(
                        V.scalar_tensor_tensor(x2f[:, osl], pyo[ot][:, :],
                                               self.b2_s[:, ot:ot + 1], xf[:, osl],
                                               op0=ALU.add, op1=ALU.add))
                e_ep = sv.n
                e_ep_prev = e_ep
                self.w(Q, sv, e_ep)
                self.lq_on(self.sr,
                           dst_dram[:, cols].rearrange("(t p) s -> p t s", p=128),
                           x2f[:, :].rearrange("p (t s) -> p t s", s=512))
                if do_ln:
                    self.w(V, sv, e_ep)
                    self.w(Sc, sv, e_ep)
                    e_ln = self.emit_ln(x2f, xb, xsq, self.xnt, ch * 512,
                                        pyo[0], pyo[1], pyo[2],
                                        xb_free=pyo_free[0])
                    pyo_free[0] = pyo_free[1] = pyo_free[2] = e_ln
        self.barrier()

    def attn_b(self):
        nc, T, Sc, V, Q = self.nc, self.T, self.S, self.V, self.Q
        st, ss, sv, sq, sg = self.st, self.ss, self.sv, self.sq, self.sg
        di = self.di
        with ExitStack() as outer:
            EO = outer.enter_context
            zkt0 = EO(nc.sbuf_tensor("zkt0", [128, S], BF16))
            zkt1 = EO(nc.sbuf_tensor("zkt1", [64, S], BF16))
            xvt0 = EO(nc.sbuf_tensor("xvt0", [128, S], BF16))
            xvt1 = EO(nc.sbuf_tensor("xvt1", [64, S], BF16))
            cosr = EO(nc.sbuf_tensor("cosr", [128, S], BF16))
            sinr = EO(nc.sbuf_tensor("sinr", [128, S], BF16))
            b16_0 = EO(nc.sbuf_tensor("b16_0", [128, D], BF16))
            b16_1 = EO(nc.sbuf_tensor("b16_1", [64, D], BF16))
            vdec0 = EO(nc.sbuf_tensor("vdec0", [128, D], BF16))
            vdec1 = EO(nc.sbuf_tensor("vdec1", [64, D], BF16))
            with ExitStack() as bb:
                E = bb.enter_context
                kp0 = E(nc.sbuf_tensor("kp0", [128, D], BF16))
                kp1 = E(nc.sbuf_tensor("kp1", [64, D], BF16))
                kn0 = E(nc.sbuf_tensor("kn0", [128, D], BF16))
                kn1 = E(nc.sbuf_tensor("kn1", [64, D], BF16))
                wkb = [E(nc.sbuf_tensor(f"wkb{t}", [128, VR], BF16)) for t in range(NT)]
                wvb = [E(nc.sbuf_tensor(f"wvb{t}", [128, VR], BF16)) for t in range(NT)]
                wqp = [E(nc.sbuf_tensor(f"wqp{t}", [128, D], BF16)) for t in range(NT)]
                p8 = [E(nc.psum_tensor(f"p8_{i}", [128, 512], F32)) for i in range(8)]
                self.new_wt_epoch()
                for t in range(NT):
                    self.lg_w(t, wkb[t][:, :], di["Wkb"][t * 128:(t + 1) * 128, :])
                    self.lg_w(8 + t, wvb[t][:, :], di["Wvb"][t * 128:(t + 1) * 128, :])
                for t in range(NT):
                    self.lg_w(16 + t, wqp[t][:, :], di["Wqb_perm"][t * 128:(t + 1) * 128, :])
                self.lq(cosr[:, :], di["COSR"][:, :])
                self.lq(sinr[:, :], di["SINR"][:, :])
                self.lg(vdec0[:, :], di["Vdec"][0:128, :])
                self.lg(vdec1[:, :], di["Vdec"][128:192, :])
                self.lg(kp0[:, :], di["KDpos"][0:128, :])
                self.lg(kp1[:, :], di["KDpos"][128:192, :])
                self.lg(kn0[:, :], di["KDneg"][0:128, :])
                e_kd = self.lg(kn1[:, :], di["KDneg"][128:192, :])
                self.w(V, sg, e_kd)
                V.tensor_tensor(b16_0[:, :], kp0[:, :], kn0[:, :], ALU.subtract)
                sv.inc(V.tensor_tensor(b16_1[:, :], kp1[:, :], kn1[:, :], ALU.subtract))
                for ch in range(8):
                    cols = slice(ch * 512, ch * 512 + 512)
                    self.w(T, sv, sv.n)
                    for kt in range(NT):
                        self.wt(kt)
                        T.matmul(p8[0][:, :], wkb[kt][:, 0:128], self.xnt[kt][:, cols],
                                 start=(kt == 0), stop=(kt == NT - 1))
                    for kt in range(NT):
                        T.matmul(p8[1][0:64, :], wkb[kt][:, 128:192], self.xnt[kt][:, cols],
                                 start=(kt == 0), stop=(kt == NT - 1))
                    for kt in range(NT):
                        self.wt(8 + kt)
                        T.matmul(p8[2][:, :], wvb[kt][:, 0:128], self.xnt[kt][:, cols],
                                 start=(kt == 0), stop=(kt == NT - 1))
                    for kt in range(NT):
                        ins = T.matmul(p8[3][0:64, :], wvb[kt][:, 128:192], self.xnt[kt][:, cols],
                                       start=(kt == 0), stop=(kt == NT - 1))
                    e_zx = st.inc(ins)
                    self.w(V, st, e_zx)
                    V.tensor_copy(zkt0[:, cols], p8[0][:, :])
                    V.tensor_copy(zkt1[:, cols], p8[1][0:64, :])
                    V.tensor_copy(xvt0[:, cols], p8[2][:, :])
                    e_zxe = sv.inc(V.tensor_copy(xvt1[:, cols], p8[3][0:64, :]))
                    self.w(T, sv, e_zxe)
                    for ot in range(NT):
                        for kt in range(NT):
                            self.wt(16 + kt)
                            ins = T.matmul(p8[ot][:, :],
                                           wqp[kt][:, ot * 128:ot * 128 + 128],
                                           self.xnt[kt][:, cols], start=(kt == 0),
                                           stop=(kt == NT - 1))
                    e_qmm = st.inc(ins)
                    self.w(V, st, e_qmm)
                    for ot in range(NT):
                        ins = V.tensor_copy(self.xnt[ot][:, cols], p8[ot][:, :])
                    sv.inc(ins)
                self.barrier()

            with ExitStack() as rp:
                E = rp.enter_context
                tA = E(nc.sbuf_tensor("ropeA", [128, S], BF16))
                tB = E(nc.sbuf_tensor("ropeB", [128, S], BF16))

                def rope_q(t):
                    self.w(V, sq, sq.n)
                    V.drain()
                    V.tensor_copy(tB[0:32, :], self.xnt[t][32:64, :])
                    V.tensor_copy(tB[32:64, :], self.xnt[t][0:32, :])
                    V.tensor_copy(tB[64:96, :], self.xnt[t][96:128, :])
                    V.tensor_copy(tB[96:128, :], self.xnt[t][64:96, :])
                    V.drain()
                    V.tensor_tensor(tB[:, :], tB[:, :], sinr[:, :], ALU.mult)
                    V.tensor_tensor(self.xnt[t][:, :], self.xnt[t][:, :],
                                    cosr[:, :], ALU.mult)
                    V.drain()
                    return sv.inc(V.tensor_tensor(self.xnt[t][:, :],
                                                  self.xnt[t][:, :], tB[:, :],
                                                  ALU.add))

                kpre = tA
                vpair = [E(nc.sbuf_tensor(f"vpair{i}", [128, 32 * 130], BF16))
                         for i in range(2)]
                expb = [E(nc.sbuf_tensor(f"expb{i}", [128, 1024], BF16)) for i in range(2)]
                ybuf = [E(nc.sbuf_tensor(f"ybuf{i}", [64, 512], BF16)) for i in range(2)]
                r_d01 = [E(nc.sbuf_tensor(f"r_d01_{i}", [1, 1024], BF16)) for i in range(2)]
                kpair = [E(nc.sbuf_tensor(f"kpair{i}", [128, S], BF16)) for i in range(2)]
                trim = E(nc.sbuf_tensor("trim", [128, 128], BF16))
                psc = [E(nc.psum_tensor(f"psc{i}", [128, 1024], F32)) for i in range(2)]
                pys = [[E(nc.psum_tensor(f"py{s_}{h_}", [128, 512], F32)) for h_ in range(2)]
                       for s_ in range(2)]

                vap = [vpair[i][:, :].rearrange("p (t c) -> p t c", c=130)
                       for i in range(2)]
                for i in range(2):
                    V.memset(vap[i][:, :, 64:65], 1.0)
                    sv.inc(V.memset(vap[i][:, :, 129:130], 1.0))

                exp_free = [0, 0]
                expb_cons = [0, 0]
                py_evac = [0, 0]
                py_recip = [0, 0]
                pw_evt = [0, 0]
                prod_bank_evt = [[0, 0], [0, 0]]
                kpair_done = [0, 0]
                vpair_done = [0, 0]
                rd_i = [0]
                nc_i = [0]
                pend_y = None
                self.lq(trim[:, :], di["TRIM"][:, :])
                self.w(T, sg, sg.n)
                self.w(V, sq, sq.n)
                rope_evt = [0] * (NT + 1)
                rope_evt[0] = rope_q(0)

                def prod_kpre2(pp, kc0, pset):
                    for i, kc in enumerate((kc0, kc0 + 1)):
                        cols = slice(kc * 512, kc * 512 + 512)
                        bank = pys[pset][i]
                        self.w(T, sv, py_evac[pset])
                        self.w(T, sv, prod_bank_evt[pset][i])
                        self.w(T, ss, py_recip[pset])
                        T.matmul(bank[:, :], b16_0[:, pp * 128:pp * 128 + 128],
                                 zkt0[:, cols], start=True, stop=False)
                        e_kp = st.inc(T.matmul(bank[:, :],
                                               b16_1[0:64, pp * 128:pp * 128 + 128],
                                               zkt1[0:64, cols],
                                               start=False, stop=True))
                        self.w(V, st, e_kp)
                        ev = sv.inc(V.tensor_copy(kpre[:, cols], bank[:, :]))
                        prod_bank_evt[pset][i] = ev
                        pw_evt[pset] = ev

                def prod_krope(pp):
                    kb = kpair[pp % 2]
                    V.drain()
                    V.tensor_copy(tB[0:32, :], kpre[32:64, :])
                    V.tensor_copy(tB[32:64, :], kpre[0:32, :])
                    V.tensor_copy(tB[64:96, :], kpre[96:128, :])
                    V.tensor_copy(tB[96:128, :], kpre[64:96, :])
                    V.drain()
                    V.tensor_tensor(kb[:, :], kpre[:, :], cosr[:, :], ALU.mult)
                    V.tensor_tensor(tB[:, :], tB[:, :], sinr[:, :], ALU.mult)
                    V.drain()
                    kpair_done[pp % 2] = sv.inc(
                        V.tensor_tensor(kb[:, :], kb[:, :], tB[:, :], ALU.add))

                def prod_vpair(pp, g0, ng, pset):
                    vb = vap[pp % 2]
                    for gi in range(ng):
                        g = g0 + gi
                        bank = pys[pset][gi % 2]
                        self.w(T, sv, py_evac[pset])
                        self.w(T, sv, prod_bank_evt[pset][gi % 2])
                        self.w(T, ss, py_recip[pset])
                        for s4 in range(4):
                            stt = g * 4 + s4
                            ssl = slice(stt * 128, stt * 128 + 128)
                            csl = slice(s4 * 128, s4 * 128 + 128)
                            T.matmul(bank[:, csl], xvt0[:, ssl],
                                     vdec0[:, pp * 128:pp * 128 + 128],
                                     start=True, stop=False)
                            ins = T.matmul(bank[:, csl], xvt1[0:64, ssl],
                                           vdec1[0:64, pp * 128:pp * 128 + 128],
                                           start=False, stop=True)
                        e_vp = st.inc(ins)
                        self.w(V, st, e_vp)
                        srcb = bank[:, :].rearrange("p (g c) -> p g c", c=128)
                        V.tensor_copy(vb[:, g * 4:g * 4 + 4, 0:64],
                                      srcb[:, :, 0:64])
                        ev = sv.inc(V.tensor_copy(vb[:, g * 4:g * 4 + 4, 65:129],
                                                  srcb[:, :, 64:128]))
                        prod_bank_evt[pset][gi % 2] = ev
                        pw_evt[pset] = ev
                    vpair_done[pp % 2] = pw_evt[pset]
                for p in range(8):
                    pp_next = p + 1 if p + 1 < 8 else None
                    if p == 0:
                        for c0 in range(4):
                            prod_kpre2(0, 2 * c0, (c0 + 1) % 2)
                        prod_krope(0)
                        prod_vpair(0, 0, 4, 1)
                        prod_vpair(0, 4, 4, 0)
                    self.w(T, sv, vpair_done[p % 2])
                    self.w(T, sv, kpair_done[p % 2])
                    self.w(T, sv, rope_evt[p])
                    def emit_tail(pp, cc, pyset, e_ylast):
                        py0, py1 = pys[pyset]
                        rb = rd_i[0]
                        rd_i[0] ^= 1
                        self.w(Sc, st, e_ylast)
                        self.w(Sc, self.sh[rb], self.sh[rb].n)
                        Sc.activation(self.r_rstd[:, :], py0[64:65, :], AF.Ln)
                        Sc.activation(self.r_var[:, :], py1[64:65, :], AF.Ln)
                        Sc.drain()
                        Sc.activation(r_d01[rb][:, 0:512], self.r_rstd[:, :], AF.Exp,
                                      scale=-1.0)
                        py_recip[pyset] = ss.inc(
                            Sc.activation(r_d01[rb][:, 512:1024], self.r_var[:, :],
                                          AF.Exp, scale=-1.0))
                        self.w(V, st, e_ylast)
                        self.w(V, self.sh[0], self.sh[0].n)
                        self.w(V, self.sh[1], self.sh[1].n)
                        V.drain()
                        V.tensor_copy(ybuf[0][:, :], py0[0:64, :])
                        py_evac[pyset] = sv.inc(V.tensor_copy(ybuf[1][:, :], py1[0:64, :]))
                        self.w(Q, sv, py_evac[pyset])
                        self.w(Q, ss, py_recip[pyset])
                        h0 = (pp * 2) * 64
                        h1 = (pp * 2 + 1) * 64
                        ccs = slice(cc * 512, cc * 512 + 512)
                        self.lq_on(self.sh[0], self.yscr[h0:h0 + 64, ccs], ybuf[0][:, :])
                        self.lq_on(self.sh[1], self.yscr[h1:h1 + 64, ccs], ybuf[1][:, :])
                        self.lq_on(self.sh[rb], self.drec[pp * 2:pp * 2 + 1, ccs],
                                   r_d01[rb][:, 0:512])
                        self.lq_on(self.sh[rb], self.drec[pp * 2 + 1:pp * 2 + 2, ccs],
                                   r_d01[rb][:, 512:1024])

                    def emit_y(pc, pt, pcol0, pbank, pexp, pyset, pmask):
                        plast = 4 * pc + 3
                        py0, py1 = pys[pyset]
                        self.w(T, ss, pexp)
                        if pmask is not None:
                            self.w(T, sv, pmask)
                        if pt == 0:
                            self.w(T, sv, py_evac[pyset])
                            self.w(T, sv, pw_evt[pyset])
                            self.w(T, ss, py_recip[pyset])
                        T.matmul(py0[0:65, pcol0:512],
                                 vpair[p % 2][:, pt * 130:pt * 130 + 65],
                                 expb[pbank][:, pcol0:512],
                                 start=(pt == 0), stop=(pt == plast))
                        ins = T.matmul(py1[0:65, pcol0:512],
                                       vpair[p % 2][:, pt * 130 + 65:pt * 130 + 130],
                                       expb[pbank][:, 512 + pcol0:1024],
                                       start=(pt == 0), stop=(pt == plast))
                        expb_cons[pbank] = st.inc(ins)

                    tiles = [(c, t) for c in range(8) for t in range(4 * c + 4)]
                    for i_f, (c, t) in enumerate(tiles):
                        m = t - 4 * c
                        col0 = 128 * m if m >= 0 else 0
                        bank = i_f % 2
                        tsl = slice(t * 128, t * 128 + 128)
                        self.w(T, ss, exp_free[bank])
                        T.matmul(psc[bank][:, col0:512],
                                 kpair[p % 2][0:64, tsl],
                                 self.xnt[p][0:64, c * 512 + col0:(c + 1) * 512],
                                 start=True, stop=(m < 0))
                        if m >= 0:
                            T.matmul(psc[bank][:, col0:col0 + 128], self.lmat[:, :],
                                     self.ident[:, :], start=False, stop=True)
                        ins = T.matmul(psc[bank][:, 512 + col0:1024],
                                       kpair[p % 2][64:128, tsl],
                                       self.xnt[p][64:128, c * 512 + col0:(c + 1) * 512],
                                       start=True, stop=(m < 0))
                        if m >= 0:
                            ins = T.matmul(psc[bank][:, 512 + col0:512 + col0 + 128],
                                           self.lmat[:, :], self.ident[:, :],
                                           start=False, stop=True)
                        e_sc = st.inc(ins)
                        self.w(Sc, st, e_sc)
                        self.w(Sc, st, expb_cons[bank])
                        if m < 0:
                            exp_free[bank] = ss.inc(Sc.activation(
                                expb[bank][:, :], psc[bank][:, :], AF.Exp, scale=0.125))
                        else:
                            Sc.activation(expb[bank][:, col0:512],
                                          psc[bank][:, col0:512], AF.Exp, scale=0.125)
                            exp_free[bank] = ss.inc(Sc.activation(
                                expb[bank][:, 512 + col0:1024],
                                psc[bank][:, 512 + col0:1024], AF.Exp, scale=0.125))
                        exp_mask = None
                        if pend_y is not None:
                            emit_y(*pend_y)
                            if pend_y[1] == 4 * pend_y[0] + 3:
                                emit_tail(p, pend_y[0], pend_y[5], st.n)
                        pend_y = (c, t, col0, bank, exp_free[bank], nc_i[0] % 2, exp_mask)
                        if pp_next is not None:
                            if t == 1 and c <= 3:
                                prod_kpre2(pp_next, 2 * c, (c + 1) % 2)
                            elif t == 2 and c == 4:
                                rope_evt[pp_next] = rope_q(pp_next)
                            elif t == 2 and c == 5:
                                prod_krope(pp_next)
                            elif t == 1 and c == 6:
                                prod_vpair(pp_next, 0, 4, (c + 1) % 2)
                            elif t == 1 and c == 7:
                                prod_vpair(pp_next, 4, 4, (c + 1) % 2)
                        if t == 4 * c + 3:
                            nc_i[0] += 1
                    emit_y(*pend_y)
                    emit_tail(p, pend_y[0], pend_y[5], st.n)
                    pend_y = None
                self.barrier()


        with ExitStack() as p3b:
            E = p3b.enter_context
            wpr = [E(nc.sbuf_tensor(f"wpr{t}", [128, D], BF16)) for t in range(NT)]
            sel2 = E(nc.sbuf_tensor("sel2", [2, 128], BF16))
            drecb = [E(nc.sbuf_tensor(f"drecb{i}", [2, 8 * 512], BF16)) for i in range(2)]
            ybT = [E(nc.sbuf_tensor(f"ybT{i}", [128, 8 * 512], BF16)) for i in range(2)]
            xf = E(nc.sbuf_tensor("p3_xf", [128, NT * 512], F32))
            x3f = E(nc.sbuf_tensor("p3_x3f", [128, NT * 512], F32))
            xb = E(nc.sbuf_tensor("p3_xb", [128, NT * 512], BF16))
            xsq = E(nc.sbuf_tensor("p3_xsq", [128, NT * 512], BF16))
            pr = [E(nc.psum_tensor(f"pr{i}", [128, 512], F32)) for i in range(2)]
            pb = [E(nc.psum_tensor(f"pb{i}", [128, 512], F32)) for i in range(2)]
            pbc = [E(nc.psum_tensor(f"pbc{i}", [128, 512], F32)) for i in range(2)]
            pst3 = E(nc.psum_tensor("pst3", [33, 512], F32))
            self.lq(sel2[:, :], di["SEL2"][:, :])
            self.new_wt_epoch()
            for t in range(NT):
                self.lg_w(t, wpr[t][:, :], di["Wproj"][t * 128:(t + 1) * 128, :])
            self.w(T, sq, sq.n)
            bcast_free = [0, 0]
            proj_done = [0, 0]
            ybn_evt = [0, 0]
            pr_free = [0, 0]
            ln3_prev = [0]
            ep_prev = [0]

            def stageA(ch):
                b = ch % 2
                cols = slice(ch * 512, ch * 512 + 512)
                self.w(Q, st, proj_done[b])
                self.lq_on(self.sh[b], ybT[b][:, :].rearrange("p (t s) -> p t s", s=512),
                           self.yscr[:, cols].rearrange("(t p) s -> p t s", p=128))
                self.lq_on(self.sh[b], drecb[b][:, :].rearrange("p (t s) -> p t s", s=512),
                           self.drec[:, cols].rearrange("(t two) s -> two t s", two=2))
                first = True
                for kt in range(NT):
                    bb = kt % 2
                    self.w(T, sv, bcast_free[bb])
                    if first:
                        self.w(T, self.sh[b], self.sh[b].n)
                        first = False
                    e_bc = st.inc(T.matmul(pbc[bb][:, :], sel2[:, :],
                                           drecb[b][:, kt * 512:kt * 512 + 512],
                                           start=True, stop=True))
                    self.w(V, st, e_bc)
                    if kt == 0:
                        self.w(V, st, proj_done[b])
                        V.drain()
                    bcast_free[bb] = sv.inc(V.tensor_tensor(
                        ybT[b][:, kt * 512:kt * 512 + 512],
                        ybT[b][:, kt * 512:kt * 512 + 512], pbc[bb][:, :], ALU.mult))
                ybn_evt[b] = sv.n

            def stageB(ch):
                b = ch % 2
                nb = (ch + 1) % 2
                pre = ch + 1 < 8
                c0 = ch * 512
                cols = slice(c0, c0 + 512)
                self.w(Q, sv, ep_prev[0])
                e_x = self.load_chunk_f32(xf, self.scr2, c0)
                if pre:
                    ncols = slice(c0 + 512, c0 + 1024)
                    self.w(Q, st, proj_done[nb])
                    self.lq_on(self.sh[nb],
                               ybT[nb][:, :].rearrange("p (t s) -> p t s", s=512),
                               self.yscr[:, ncols].rearrange("(t p) s -> p t s", p=128))
                    self.lq_on(self.sh[nb],
                               drecb[nb][:, :].rearrange("p (t s) -> p t s", s=512),
                               self.drec[:, ncols].rearrange("(t two) s -> two t s",
                                                             two=2))
                self.w(T, sv, ybn_evt[b])
                for ot in range(NT):
                    pb_ = ot % 2
                    self.w(T, sv, pr_free[pb_])
                    for kt in range(NT):
                        self.wt(kt)
                        ins = T.matmul(pr[pb_][:, :], wpr[kt][:, ot * 128:ot * 128 + 128],
                                       ybT[b][:, kt * 512:kt * 512 + 512],
                                       start=(kt == 0), stop=(kt == NT - 1))
                    e_o = st.inc(ins)
                    if pre:
                        self.w(T, sv, bcast_free[pb_])
                        if ot == 0:
                            self.w(T, self.sh[nb], self.sh[nb].n)
                        e_bc = st.inc(T.matmul(pbc[pb_][:, :], sel2[:, :],
                                               drecb[nb][:, ot * 512:ot * 512 + 512],
                                               start=True, stop=True))
                    self.w(V, st, e_o)
                    self.w(V, self.sx, self.sx.n)
                    self.w(V, self.sr, self.sr.n)
                    if ot == 0:
                        V.drain()
                    osl = slice(ot * 512, ot * 512 + 512)
                    pr_free[pb_] = sv.inc(V.tensor_tensor(x3f[:, osl], pr[pb_][:, :],
                                                          xf[:, osl], ALU.add))
                    if pre:
                        self.w(V, st, e_bc)
                        bcast_free[pb_] = sv.inc(V.tensor_tensor(
                            ybT[nb][:, ot * 512:ot * 512 + 512],
                            ybT[nb][:, ot * 512:ot * 512 + 512],
                            pbc[pb_][:, :], ALU.mult))
                if pre:
                    ybn_evt[nb] = sv.n
                proj_done[b] = st.n
                e_ep = sv.n
                ep_prev[0] = e_ep
                self.w(Q, sv, e_ep)
                self.lq_on(self.sr,
                           self.scr3[:, cols].rearrange("(t p) s -> p t s", p=128),
                           x3f[:, :].rearrange("p (t s) -> p t s", s=512))
                self.w(V, sv, e_ep)
                self.w(Sc, sv, e_ep)
                ln3_prev[0] = self.emit_ln(x3f, xb, xsq, self.xnt, c0, pst3,
                                           pb[0], pb[1], xb_free=ln3_prev[0])

            stageA(0)
            for ch in range(8):
                stageB(ch)
        self.barrier()

    def finish(self):
        self.Q.wait_ge(self.sq.h, self.sq.n)
        self.Q.wait_ge(self.sr.h, self.sr.n)
        self.Q.wait_ge(self.sh[0].h, self.sh[0].n)
        self.Q.wait_ge(self.sh[1].h, self.sh[1].n)
        self.G.wait_ge(self.sg.h, self.sg.n)


def build_program():
    nc = bass.Bass("TRN2", target_bir_lowering=False)
    with ExitStack() as stack:
        mk = MK(nc, stack)
        mk.phase0()
        if _PHASES >= 1:
            mk.attn_a()
        if _PHASES >= 2:
            mk.ffn(mk.scr1, mk.scr2, do_ln=True, tag="f1")
        if _PHASES >= 3:
            mk.attn_b()
        if _PHASES >= 4:
            mk.ffn(mk.scr3, mk.d_out, do_ln=False, tag="f2")
        else:
            src = {0: mk.dxT, 1: mk.scr1, 2: mk.scr2, 3: mk.scr3}[min(_PHASES, 3)]
            with ExitStack() as dbg:
                buf = dbg.enter_context(nc.sbuf_tensor("dbgbuf", [128, 4096], F32))
                for ch in range(8):
                    cols = slice(ch * 512, ch * 512 + 512)
                    mk.w(mk.Q, mk.sq, mk.sq.n)
                    mk.lq(buf[:, :].rearrange("p (t s) -> p t s", s=512),
                          src[:, cols].rearrange("(t p) s -> p t s", p=128))
                    mk.w(mk.Q, mk.sq, mk.sq.n)
                    mk.lq(mk.d_out[:, cols].rearrange("(t p) s -> p t s", p=128),
                          buf[:, :].rearrange("p (t s) -> p t s", s=512))
        mk.finish()
    return nc



def _host_prep(inputs):
    g = lambda n: np.ascontiguousarray(np.asarray(inputs[n], dtype=np.float32))
    x = g("L0_attn_a_x")
    xT = np.ascontiguousarray(x.transpose(0, 2, 1))

    shared = {}
    shared["WqaT"] = np.ascontiguousarray(
        np.concatenate([g("L0_attn_a_Wq").T, g("L0_attn_a_bq")[None, :]], axis=0))
    shared["bqa"] = np.ascontiguousarray(g("L0_attn_a_bq").reshape(NT, 128).T)
    shared["WkaT"] = np.ascontiguousarray(
        np.concatenate([g("L0_attn_a_Wk").T, g("L0_attn_a_bk")[None, :]], axis=0))
    shared["WvaT"] = np.ascontiguousarray(
        np.concatenate([g("L0_attn_a_Wv").T, g("L0_attn_a_bv")[None, :]], axis=0))
    shared["WoaT"] = np.ascontiguousarray(g("L0_attn_a_Wo").T)
    shared["boa"] = np.ascontiguousarray(g("L0_attn_a_bo")[None, :])
    shared["w1"] = g("L0_ffn_w1")
    shared["b1c"] = np.ascontiguousarray(g("L0_ffn_b1").reshape(NJ, 128).T)
    shared["w2"] = g("L0_ffn_w2")
    shared["b2c"] = np.ascontiguousarray(g("L0_ffn_b2").reshape(NT, 128).T)

    perm = np.zeros(D, dtype=np.int64)
    for h in range(H):
        base = (h // 2) * 128 + (h % 2) * 64
        for j in range(FF):
            perm[base + j] = h * HD + 2 * j
            perm[base + 32 + j] = h * HD + 2 * j + 1
    shared["Wqb_perm"] = np.ascontiguousarray(g("L0_attn_b_Wq")[:, perm])
    shared["Wkb"] = g("L0_attn_b_Wk")
    shared["Wvb"] = g("L0_attn_b_Wv")
    kd = g("L0_attn_b_key_decoder")
    br_, bi_ = kd[..., 0], kd[..., 1]
    KDpos = np.zeros((VR, D), np.float32)
    KDneg = np.zeros((VR, D), np.float32)
    for f in range(FF):
        for r in range(KR):
            for h in range(H):
                c1 = (h // 2) * 128 + (h % 2) * 64 + f
                c2 = (h // 2) * 128 + (h % 2) * 64 + 32 + f
                KDpos[f * 6 + r * 2 + 0, c1] = br_[f, h, r]
                KDpos[f * 6 + r * 2 + 0, c2] = bi_[f, h, r]
                KDpos[f * 6 + r * 2 + 1, c2] = br_[f, h, r]
                KDneg[f * 6 + r * 2 + 1, c1] = bi_[f, h, r]
    shared["KDpos"] = KDpos
    shared["KDneg"] = KDneg
    shared["Vdec"] = np.ascontiguousarray(
        g("L0_attn_b_value_decoder").transpose(1, 0, 2).reshape(VR, D))
    shared["Wproj"] = g("L0_attn_b_Wproj")

    half = HD // 2
    freq = 1.0 / 10000.0 ** (np.arange(half, dtype=np.float32) / half)
    freqs = np.outer(np.arange(S, dtype=np.float32), freq)
    cos = np.cos(freqs).astype(np.float32)
    sin = np.sin(freqs).astype(np.float32)
    p_idx = np.arange(128)
    jmod = p_idx % 32
    sign = np.where((p_idx % 64) < 32, -1.0, 1.0).astype(np.float32)
    shared["COSR"] = np.ascontiguousarray(cos.T[jmod]).astype(BFNP)
    shared["SINR"] = np.ascontiguousarray(sin.T[jmod] * sign[:, None]).astype(BFNP)
    TRI = np.triu(np.ones((128, 128), np.float32))
    shared["LMAT"] = np.ascontiguousarray(((-1e9) * (1.0 - TRI)).astype(BFNP).T)
    shared["IDENT"] = np.eye(128, dtype=BFNP)
    sel2 = np.zeros((2, 128), dtype=BFNP)
    sel2[0, 0:64] = 1
    sel2[1, 64:128] = 1
    shared["SEL2"] = sel2
    shared["TRIM"] = TRI.astype(BFNP)
    return xT, shared


def kernel(**inputs):
    xT, shared = _host_prep(inputs)
    nc = build_program()
    in_maps = [dict(shared, xT=np.ascontiguousarray(xT[c])) for c in range(NC)]
    res = run_bass_kernel_spmd(nc, in_maps, core_ids=list(range(NC)))
    out = np.stack([np.asarray(r["outT"], np.float32).T for r in res.results], axis=0)
    return np.ascontiguousarray(out)


# version 22 (final)
# speedup vs baseline: 1.0162x; 1.0162x over previous
import numpy as np
import ml_dtypes
from contextlib import ExitStack

import concourse.bass as bass
import concourse.mybir as mybir
from concourse.bass_utils import run_bass_kernel_spmd

F32 = mybir.dt.float32
BF16 = mybir.dt.bfloat16
AF = mybir.ActivationFunctionType
ALU = mybir.AluOpType
BFNP = ml_dtypes.bfloat16

D = 1024
S = 4096
NT = 8
NC = 8
H = 16
HD = 64
FF = 32
KR = 3
VR = 192
NJ = 32
EPS = 1e-5

_PHASES = 99


class Ctr:
    __slots__ = ("h", "n")

    def __init__(self, h):
        self.h = h
        self.n = 0

    def inc(self, instr, k=1):
        instr.then_inc(self.h, k)
        self.n += k
        return self.n


class MK:
    def __init__(self, nc, stack):
        self.nc = nc
        self.stack = stack
        E = stack.enter_context
        self.T, self.S, self.V, self.Q, self.G = nc.tensor, nc.scalar, nc.vector, nc.sync, nc.gpsimd

        self.dxT = nc.dram_tensor("xT", [D, S], F32, kind="ExternalInput")
        self.d_out = nc.dram_tensor("outT", [D, S], F32, kind="ExternalOutput")
        self.scr1 = nc.dram_tensor("scr1", [D, S], F32, kind="Internal")
        self.scr2 = nc.dram_tensor("scr2", [D, S], F32, kind="Internal")
        self.scr3 = nc.dram_tensor("scr3", [D, S], F32, kind="Internal")
        self.scr_h = nc.dram_tensor("scr_h", [4096, S], BF16, kind="Internal")
        self.yscr = nc.dram_tensor("yscr", [D, S], BF16, kind="Internal")
        self.drec = nc.dram_tensor("drec", [H, S], BF16, kind="Internal")
        self.d_ksum = nc.dram_tensor("d_ksum", [D], BF16, kind="Internal")
        di = {}
        for name, shape in [
            ("WqaT", [D + 1, D]), ("bqa", [128, NT]), ("WkaT", [D + 1, D]),
            ("WvaT", [D + 1, D]), ("WoaT", [D, D]), ("boa", [1, D]),
            ("w1", [D, 4096]), ("b1c", [128, NJ]), ("w2", [4096, D]),
            ("b2c", [128, NT]), ("Wqb_perm", [D, D]), ("Wkb", [D, VR]),
            ("Wvb", [D, VR]), ("KDpos", [VR, D]), ("KDneg", [VR, D]),
            ("Vdec", [VR, D]), ("Wproj", [D, D]),
        ]:
            di[name] = nc.dram_tensor(name, shape, F32, kind="ExternalInput")
        for name, shape in [("COSR", [128, S]), ("SINR", [128, S]),
                            ("LMAT", [128, 128]), ("IDENT", [128, 128]),
                            ("SEL2", [2, 128]), ("TRIM", [128, 128])]:
            di[name] = nc.dram_tensor(name, shape, BF16, kind="ExternalInput")
        self.di = di

        self.xnt = [E(nc.sbuf_tensor(f"xnt{t}", [128, S], BF16)) for t in range(NT)]
        self.lmat = E(nc.sbuf_tensor("s_lmat", [128, 128], BF16))
        self.ident = E(nc.sbuf_tensor("s_ident", [128, 128], BF16))
        self.ones_r16 = E(nc.sbuf_tensor("ones_r16", [1, 128], BF16))
        self.ones_c16 = E(nc.sbuf_tensor("ones_c16", [128, 1], BF16))
        self.ones_rf = E(nc.sbuf_tensor("ones_rf", [1, 128], F32))
        self.ones512 = E(nc.sbuf_tensor("ones512", [1, 512], BF16))
        self.bqa_s = E(nc.sbuf_tensor("bqa_s", [128, NT], F32))
        self.b1_s = E(nc.sbuf_tensor("b1_s", [128, NJ], F32))
        self.b2_s = E(nc.sbuf_tensor("b2_s", [128, NT], F32))
        self.boa16 = E(nc.sbuf_tensor("s_boa16", [1, D], BF16))
        self.r_mean = E(nc.sbuf_tensor("r_mean", [1, 512], F32))
        self.r_var = E(nc.sbuf_tensor("r_var", [1, 512], F32))
        self.r_rstd = E(nc.sbuf_tensor("r_rstd", [1, 512], F32))
        self.tln = E(nc.sbuf_tensor("s_tln", [128, 512], F32))

        self.st = Ctr(E(nc.semaphore("st")))
        self.ss = Ctr(E(nc.semaphore("ss")))
        self.sv = Ctr(E(nc.semaphore("sv")))
        self.sq = Ctr(E(nc.semaphore("sq")))
        self.sg = Ctr(E(nc.semaphore("sg")))
        self.sr = Ctr(E(nc.semaphore("sr")))
        self.sh = [Ctr(E(nc.semaphore("sh0"))), Ctr(E(nc.semaphore("sh1")))]
        self.sx = Ctr(E(nc.semaphore("sx")))
        self.shh = [Ctr(E(nc.semaphore(f"shh{i}"))) for i in range(4)]
        self.swt = [Ctr(E(nc.semaphore(f"swt{i}"))) for i in range(33)]
        self._wt_waited = set()
        self._wt_epoch = 0

        self.lq(self.lmat[:, :], di["LMAT"][:, :])
        self.lq(self.ident[:, :], di["IDENT"][:, :])
        self.lq(self.bqa_s[:, :], di["bqa"][:, :])
        self.lq(self.b1_s[:, :], di["b1c"][:, :])
        self.lq(self.b2_s[:, :], di["b2c"][:, :])
        self.lg(self.boa16[:, :], di["boa"][:, :])
        self.V.memset(self.ones_r16[:, :], 1.0)
        self.V.memset(self.ones512[:, :], 1.0)
        self.V.memset(self.ones_c16[:, :], 1.0)
        self.V.memset(self.tln[:, :], 0.0)
        self.sv.inc(self.V.memset(self.ones_rf[:, :], 1.0))
        self.init_v = self.sv.n
        self.V.drain()

    def w(self, eng, c, v):
        assert 0 <= v <= c.n, f"wait target {v} > emitted {c.n}"
        if v > 0:
            eng.wait_ge(c.h, v)

    def _ninstr(self):
        return sum(len(b.instructions) for b in self.nc.m.functions[0].basic_blocks) \
            if False else sum(1 for _ in self.nc.all_instructions())

    def lq(self, dst, src):
        n0 = self._ninstr()
        ins = self.Q.dma_start(dst, src)
        k = self._ninstr() - n0
        ins.then_inc(self.sq.h, 16)
        self.sq.n += 16 * k
        return self.sq.n

    def lg(self, dst, src):
        n0 = self._ninstr()
        ins = self.G.dma_start(dst, src)
        k = self._ninstr() - n0
        ins.then_inc(self.sg.h, 16)
        self.sg.n += 16 * k
        return self.sg.n

    def lg_w(self, idx, dst, src):
        c = self.swt[idx]
        n0 = self._ninstr()
        ins = self.G.dma_start(dst, src)
        k = self._ninstr() - n0
        ins.then_inc(c.h, 16)
        c.n += 16 * k
        return c.n

    def wt(self, idx):
        key = (self._wt_epoch, idx)
        if key not in self._wt_waited:
            self.w(self.T, self.swt[idx], self.swt[idx].n)
            self._wt_waited.add(key)

    def new_wt_epoch(self):
        self._wt_epoch += 1

    def lq_on(self, ctr, dst, src):
        n0 = self._ninstr()
        ins = self.Q.dma_start(dst, src)
        k = self._ninstr() - n0
        ins.then_inc(ctr.h, 16)
        ctr.n += 16 * k
        return ctr.n

    def barrier(self):
        T, Sc, V, Q, G = self.T, self.S, self.V, self.Q, self.G
        vv = self.sv.inc(V.tensor_copy(self.tln[0:1, 0:4], self.tln[0:1, 0:4]))
        sw = self.ss.inc(Sc.copy(self.tln[0:1, 4:8], self.tln[0:1, 4:8]))
        tv = self.st.n
        for eng in (T, Sc, V, Q, G):
            self.w(eng, self.st, tv)
            self.w(eng, self.sv, vv)
            self.w(eng, self.ss, sw)
            self.w(eng, self.sq, self.sq.n)
            self.w(eng, self.sg, self.sg.n)
            self.w(eng, self.sr, self.sr.n)
            self.w(eng, self.sx, self.sx.n)
            self.w(eng, self.sh[0], self.sh[0].n)
            self.w(eng, self.sh[1], self.sh[1].n)
            for _c in self.shh:
                self.w(eng, _c, _c.n)

    def load_chunk_f32(self, dst, dram, c0, width=512, ctr=None):
        return self.lq_on(
            self.sx if ctr is None else ctr,
            dst[:, :].rearrange("p (t s) -> p t s", s=width),
            dram[:, c0:c0 + width].rearrange("(t p) s -> p t s", p=128))

    def emit_ln(self, src, xb, xsq, dst_tiles, dc0, pstats, pbm, pbr, rows=None,
                xb_free=0):
        T, Sc, V = self.T, self.S, self.V
        st, ss, sv = self.st, self.ss, self.sv
        r_mean, r_var, r_rstd = rows if rows is not None else (
            self.r_mean, self.r_var, self.r_rstd)
        w = 512
        V.drain()
        e_cast = sv.inc(V.tensor_copy(xb[:, :], src[:, :]))
        e_sq = ss.inc(Sc.activation(xsq[:, :], src[:, :], AF.Square))
        self.w(T, sv, e_cast)
        self.w(T, ss, e_sq)
        for t in range(NT):
            T.matmul(pstats[0:1, :], self.ones_c16[:, :], xb[:, t * w:(t + 1) * w],
                     start=(t == 0), stop=(t == NT - 1))
        for t in range(NT):
            ins = T.matmul(pstats[32:33, :], self.ones_c16[:, :], xsq[:, t * w:(t + 1) * w],
                           start=(t == 0), stop=(t == NT - 1))
        e_stats = st.inc(ins)
        self.w(V, st, e_stats)
        V.tensor_single_scalar(r_mean[:, :], pstats[0:1, :], 1.0 / D, ALU.mult)
        V.tensor_scalar(r_var[:, :], pstats[32:33, :], 1.0 / D, EPS,
                        op0=ALU.mult, op1=ALU.add)
        V.drain()
        V.tensor_tensor(r_rstd[:, :], r_mean[:, :], r_mean[:, :], ALU.mult)
        V.drain()
        e_rows = sv.inc(V.tensor_tensor(r_var[:, :], r_var[:, :],
                                        r_rstd[:, :], ALU.subtract))
        self.w(Sc, sv, e_rows)
        Sc.activation(r_rstd[:, :], r_var[:, :], AF.Ln)
        Sc.drain()
        e_rstd = ss.inc(Sc.activation(r_rstd[:, :], r_rstd[:, :],
                                      AF.Exp, scale=-0.5))
        self.w(T, sv, e_rows)
        self.w(T, ss, e_rstd)
        T.matmul(pbm[:, :], self.ones_rf[:, :], r_mean[:, :], start=True, stop=True)
        e_bc = st.inc(T.matmul(pbr[:, :], self.ones_rf[:, :], r_rstd[:, :],
                               start=True, stop=True))
        self.w(V, st, e_bc)
        for t in range(NT):
            V.tensor_tensor(xb[:, t * w:(t + 1) * w], src[:, t * w:(t + 1) * w],
                            pbm[:, :], ALU.subtract)
        V.drain()
        for t in range(NT):
            ins = V.tensor_tensor(dst_tiles[t][:, dc0:dc0 + w], xb[:, t * w:(t + 1) * w],
                                  pbr[:, :], ALU.mult)
        return sv.inc(ins)

    def phase0(self):
        nc, T, Sc, V, Q = self.nc, self.T, self.S, self.V, self.Q
        with ExitStack() as p:
            E = p.enter_context
            xf = [E(nc.sbuf_tensor(f"p0_xf{i}", [128, NT * 512], F32)) for i in range(2)]
            xb = [E(nc.sbuf_tensor(f"p0_xb{i}", [128, NT * 512], BF16)) for i in range(2)]
            xsq = [E(nc.sbuf_tensor(f"p0_xsq{i}", [128, NT * 512], BF16)) for i in range(2)]
            rows = [(E(nc.sbuf_tensor(f"p0_rm{i}", [1, 512], F32)),
                     E(nc.sbuf_tensor(f"p0_rv{i}", [1, 512], F32)),
                     E(nc.sbuf_tensor(f"p0_rs{i}", [1, 512], F32))) for i in range(2)]
            pstats = [E(nc.psum_tensor(f"p0_pstats{i}", [33, 512], F32)) for i in range(2)]
            pbm = [E(nc.psum_tensor(f"p0_pbm{i}", [128, 512], F32)) for i in range(2)]
            pbr = [E(nc.psum_tensor(f"p0_pbr{i}", [128, 512], F32)) for i in range(2)]
            ln_done = [0, 0]
            for ch in range(8):
                b = ch % 2
                c0 = ch * 512
                self.w(Q, self.sv, ln_done[b])
                e_x = self.load_chunk_f32(xf[b], self.dxT, c0, ctr=self.sh[b])
                self.w(V, self.sh[b], e_x)
                self.w(Sc, self.sh[b], e_x)
                ln_done[b] = self.emit_ln(xf[b], xb[b], xsq[b], self.xnt, c0,
                                          pstats[b], pbm[b], pbr[b], rows=rows[b],
                                          xb_free=ln_done[b])
        self.barrier()

    def attn_a(self):
        nc, T, Sc, V, Q = self.nc, self.T, self.S, self.V, self.Q
        st, ss, sv, sq, sg = self.st, self.ss, self.sv, self.sq, self.sg
        di = self.di
        with ExitStack() as outer:
            EO = outer.enter_context
            KV16 = [EO(nc.sbuf_tensor(f"KV16_{t}", [128, D], BF16)) for t in range(NT)]
            ksumT = EO(nc.sbuf_tensor("ksumT", [128, NT], BF16))

            with ExitStack() as pa:
                E = pa.enter_context
                wka = [E(nc.sbuf_tensor(f"wka{t}", [128, D], BF16)) for t in range(NT)]
                wka_b = E(nc.sbuf_tensor("wka_b", [1, D], BF16))
                wva = [E(nc.sbuf_tensor(f"wva{t}", [128, D], BF16)) for t in range(NT)]
                wva_b = E(nc.sbuf_tensor("wva_b", [1, D], BF16))
                kbuf = [E(nc.sbuf_tensor(f"kbuf{t}", [128, D], BF16)) for t in range(NT)]
                vbuf = [E(nc.sbuf_tensor(f"vbuf{t}", [128, D], BF16)) for t in range(NT)]
                KV32 = [E(nc.sbuf_tensor(f"KV32_{t}", [128, D], F32)) for t in range(NT)]
                tm = [E(nc.sbuf_tensor(f"tm{i}", [128, 512], F32)) for i in range(2)]
                r_ks = E(nc.sbuf_tensor("r_ks", [1, D], F32))
                r_ks16 = E(nc.sbuf_tensor("r_ks16", [1, D], BF16))
                ppk = [E(nc.psum_tensor(f"ppk{i}", [128, 512], F32)) for i in range(2)]
                ppv = [E(nc.psum_tensor(f"ppv{i}", [128, 512], F32)) for i in range(2)]
                pkv = [E(nc.psum_tensor(f"pkv{i}", [128, 512], F32)) for i in range(2)]
                pks = [E(nc.psum_tensor(f"pks{i}", [1, 512], F32)) for i in range(2)]

                self.new_wt_epoch()
                for t in range(NT):
                    self.lg_w(t, wka[t][:, :], di["WkaT"][t * 128:(t + 1) * 128, :])
                    self.lg_w(8 + t, wva[t][:, :], di["WvaT"][t * 128:(t + 1) * 128, :])
                self.lg_w(16, wka_b[:, :], di["WkaT"][D:D + 1, :])
                self.lg_w(17, wva_b[:, :], di["WvaT"][D:D + 1, :])

                k_free = [0, 0]
                v_free = [0, 0]
                kv_free = [0, 0]
                kbuf_cons = 0
                tm_free = [0, 0]
                te_free = [0, 0]
                ei = 0
                for ch in range(4):
                    for tt in range(NT):
                        gt = ch * 8 + tt
                        sl = slice(gt * 128, gt * 128 + 128)
                        for half in range(2):
                            hs = slice(half * 512, half * 512 + 512)
                            self.w(T, sv, k_free[half])
                            for kt in range(NT):
                                self.wt(kt)
                                T.matmul(ppk[half][:, :], self.xnt[kt][:, sl], wka[kt][:, hs],
                                         start=(kt == 0), stop=False)
                            self.wt(16)
                            e_k = st.inc(T.matmul(ppk[half][:, :], self.ones_r16[:, 0:128],
                                                  wka_b[:, hs], start=False, stop=True))
                            self.w(T, sv, v_free[half])
                            for kt in range(NT):
                                self.wt(8 + kt)
                                T.matmul(ppv[half][:, :], self.xnt[kt][:, sl], wva[kt][:, hs],
                                         start=(kt == 0), stop=False)
                            self.wt(17)
                            e_v = st.inc(T.matmul(ppv[half][:, :], self.ones_r16[:, 0:128],
                                                  wva_b[:, hs], start=False, stop=True))
                            self.w(V, st, e_k)
                            self.w(V, ss, tm_free[ei])
                            self.w(V, st, kbuf_cons)
                            self.w(V, sv, te_free[ei])
                            V.drain()
                            e_m = sv.inc(V.tensor_single_scalar(tm[ei][:, :], ppk[half][:, :],
                                                                0.0, ALU.min))
                            self.w(Sc, sv, e_m)
                            tm_free[ei] = ss.inc(Sc.activation(tm[ei][:, :], tm[ei][:, :], AF.Exp))
                            self.w(V, ss, tm_free[ei])
                            k_free[half] = sv.inc(V.scalar_tensor_tensor(
                                kbuf[tt][:, hs], ppk[half][:, :], 0.0, tm[ei][:, :],
                                op0=ALU.max, op1=ALU.add))
                            te_free[ei] = k_free[half]
                            ei ^= 1
                            self.w(V, st, e_v)
                            self.w(V, st, kbuf_cons)
                            v_free[half] = sv.inc(V.tensor_copy(vbuf[tt][:, hs], ppv[half][:, :]))
                    e_tiles = sv.n
                    self.w(T, sv, e_tiles)
                    for dt in range(NT):
                        for eh in range(2):
                            b = (dt * 2 + eh) % 2
                            self.w(T, sv, kv_free[b])
                            for tt in range(NT):
                                ins = T.matmul(pkv[b][:, :], kbuf[tt][:, dt * 128:dt * 128 + 128],
                                               vbuf[tt][:, eh * 512:eh * 512 + 512],
                                               start=(tt == 0), stop=(tt == NT - 1))
                            e_kv = st.inc(ins)
                            self.w(V, st, e_kv)
                            if ch == 0:
                                kv_free[b] = sv.inc(V.tensor_copy(
                                    KV32[dt][:, eh * 512:eh * 512 + 512], pkv[b][:, :]))
                            else:
                                V.drain()
                                kv_free[b] = sv.inc(V.tensor_tensor(
                                    KV32[dt][:, eh * 512:eh * 512 + 512],
                                    KV32[dt][:, eh * 512:eh * 512 + 512],
                                    pkv[b][:, :], ALU.add))
                    for tt in range(NT):
                        for eh in range(2):
                            ins = T.matmul(pks[eh][:, :], self.ones_c16[:, :],
                                           kbuf[tt][:, eh * 512:eh * 512 + 512],
                                           start=(ch == 0 and tt == 0),
                                           stop=(ch == 3 and tt == NT - 1))
                    kbuf_cons = st.inc(ins)

                self.w(V, st, st.n)
                V.drain()
                V.tensor_single_scalar(r_ks[:, 0:512], pks[0][:, :], 1e-6, ALU.add)
                V.tensor_single_scalar(r_ks[:, 512:1024], pks[1][:, :], 1e-6, ALU.add)
                V.drain()
                e_ksf = sv.inc(V.tensor_copy(r_ks16[:, :], r_ks[:, :]))
                self.w(Q, sv, e_ksf)
                e_st = self.lq(self.d_ksum[:].rearrange("(o d) -> o d", o=1), r_ks16[:, :])
                self.w(Q, sq, e_st)
                for t in range(NT):
                    self.lq(ksumT[:, t:t + 1],
                            self.d_ksum[t * 128:(t + 1) * 128].rearrange("(p o) -> p o", o=1))
                V.drain()
                for dt in range(NT):
                    sv.inc(V.tensor_copy(KV16[dt][:, :], KV32[dt][:, :]))
                self.barrier()

            with ExitStack() as pb:
                E = pb.enter_context
                wqa = [E(nc.sbuf_tensor(f"wqa{t}", [128, D], BF16)) for t in range(NT)]
                wqa_b = E(nc.sbuf_tensor("wqa_b", [1, D], BF16))
                woa = [E(nc.sbuf_tensor(f"woa{t}", [128, D], BF16)) for t in range(NT)]
                qch = [E(nc.sbuf_tensor(f"qch{t}", [128, 512], BF16)) for t in range(NT)]
                cch = [E(nc.sbuf_tensor(f"cch{t}", [128, 512], BF16)) for t in range(NT)]
                xf = E(nc.sbuf_tensor("p1_xf", [128, NT * 512], F32))
                x1f = E(nc.sbuf_tensor("p1_x1f", [128, NT * 512], F32))
                xb = E(nc.sbuf_tensor("p1_xb", [128, NT * 512], BF16))
                xsq = E(nc.sbuf_tensor("p1_xsq", [128, NT * 512], BF16))
                tm = [E(nc.sbuf_tensor(f"tm1_{i}", [128, 512], F32)) for i in range(2)]
                zbs = E(nc.sbuf_tensor("zbs", [128, 512], F32))
                r_z = self.r_mean
                r_rz = self.r_var
                r_z16 = E(nc.sbuf_tensor("r_z16", [1, 512], BF16))
                ppa = [E(nc.psum_tensor(f"ppa{i}", [128, 512], F32)) for i in range(2)]
                ppb = [E(nc.psum_tensor(f"ppb{i}", [128, 512], F32)) for i in range(2)]
                ppt = [E(nc.psum_tensor(f"ppt{i}", [128, 512], F32)) for i in range(2)]
                pzb = E(nc.psum_tensor("pzb", [128, 512], F32))
                pstats = E(nc.psum_tensor("p1_pstats", [33, 512], F32))

                self.new_wt_epoch()
                for t in range(NT):
                    self.lg_w(t, wqa[t][:, :], di["WqaT"][t * 128:(t + 1) * 128, :])
                self.lg_w(16, wqa_b[:, :], di["WqaT"][D:D + 1, :])
                for t in range(NT):
                    self.lg_w(8 + t, woa[t][:, :], di["WoaT"][t * 128:(t + 1) * 128, :])
                self.w(T, sq, sq.n)

                ei = 0
                qfree = [0, 0]
                cfree = [0, 0]
                ep_evt = [0] * NT
                for ch in range(8):
                    c0 = ch * 512
                    cols = slice(c0, c0 + 512)
                    self.w(Q, sv, sv.n if ch > 0 else 0)
                    e_x = self.load_chunk_f32(xf, self.dxT, c0)
                    for ot in range(NT):
                        b = ot % 2
                        self.w(T, sv, qfree[b])
                        for kt in range(NT):
                            self.wt(kt)
                            T.matmul(ppa[b][:, :], wqa[kt][:, ot * 128:ot * 128 + 128],
                                     self.xnt[kt][:, cols], start=(kt == 0), stop=False)
                        self.wt(16)
                        e_q = st.inc(T.matmul(
                            ppa[b][:, :], wqa_b[:, ot * 128:ot * 128 + 128],
                            self.ones512[:, :], start=False, stop=True))
                        self.w(V, st, e_q)
                        V.drain()
                        e_m = sv.inc(V.tensor_single_scalar(tm[ei][:, :], ppa[b][:, :],
                                                            0.0, ALU.min))
                        self.w(Sc, sv, e_m)
                        e_e = ss.inc(Sc.activation(tm[ei][:, :], tm[ei][:, :], AF.Exp))
                        self.w(V, ss, e_e)
                        qfree[b] = sv.inc(
                            V.scalar_tensor_tensor(qch[ot][:, :], ppa[b][:, :], 0.0,
                                                   tm[ei][:, :], op0=ALU.max, op1=ALU.add))
                        ei ^= 1
                    e_qdone = sv.n
                    self.w(T, sv, e_qdone)
                    for dt in range(NT):
                        ins = T.matmul(pstats[0:1, :], ksumT[:, dt:dt + 1], qch[dt][:, :],
                                       start=(dt == 0), stop=(dt == NT - 1))
                    e_zmm = st.inc(ins)
                    self.w(V, st, e_zmm)
                    V.drain()
                    V.tensor_copy(r_z[:, :], pstats[0:1, :])
                    V.drain()
                    V.reciprocal(r_rz[:, :], r_z[:, :])
                    e_z = sv.inc(V.tensor_copy(r_z16[:, :], r_z[:, :]))
                    for et in range(NT):
                        b = et % 2
                        self.w(T, sv, cfree[b])
                        for dt in range(NT):
                            ins = T.matmul(ppb[b][:, :], KV16[dt][:, et * 128:et * 128 + 128],
                                           qch[dt][:, :], start=(dt == 0), stop=(dt == NT - 1))
                        e_c = st.inc(ins)
                        self.w(V, st, e_c)
                        cfree[b] = sv.inc(V.tensor_copy(cch[et][:, :], ppb[b][:, :]))
                    e_cdone = sv.n
                    self.w(T, sv, e_cdone)
                    self.w(T, sv, e_z)
                    e_zb = st.inc(T.matmul(pzb[:, :], self.ones_rf[:, :], r_rz[:, :],
                                           start=True, stop=True))
                    self.w(V, st, e_zb)
                    e_zbs = sv.inc(V.tensor_copy(zbs[:, :], pzb[:, :]))
                    e_t1 = [0] * NT
                    for ot in range(NT):
                        b = ot % 2
                        self.w(T, sv, ep_evt[ot - 2] if ot >= 2 else ep_evt[NT - 2 + ot])
                        for et in range(NT):
                            self.wt(8 + et)
                            T.matmul(ppt[b][:, :], woa[et][:, ot * 128:ot * 128 + 128],
                                     cch[et][:, :], start=(et == 0), stop=False)
                        e_t1[ot] = st.inc(T.matmul(ppt[b][:, :], self.boa16[:, ot * 128:ot * 128 + 128],
                                                   r_z16[:, :], start=False, stop=True))
                        self.w(V, st, e_t1[ot])
                        osl = slice(ot * 512, ot * 512 + 512)
                        if ot == 0:
                            self.w(V, self.sr, self.sr.n)
                            V.drain()
                        ep_evt[ot] = sv.inc(
                            V.tensor_tensor(x1f[:, osl], ppt[b][:, :], zbs[:, :], ALU.mult))
                    self.w(V, self.sx, self.sx.n)
                    V.drain()
                    for ot in range(NT):
                        osl = slice(ot * 512, ot * 512 + 512)
                        sv.inc(V.tensor_tensor(x1f[:, osl], x1f[:, osl], xf[:, osl], ALU.add))
                    e_ep = sv.n
                    self.w(Q, sv, e_ep)
                    self.lq_on(self.sr,
                               self.scr1[:, cols].rearrange("(t p) s -> p t s", p=128),
                               x1f[:, :].rearrange("p (t s) -> p t s", s=512))
                    self.w(V, sv, e_ep)
                    self.w(Sc, sv, e_ep)
                    e_ln = self.emit_ln(x1f, xb, xsq, self.xnt, c0, pstats,
                                        ppt[0], ppt[1], xb_free=ep_evt[NT - 2])
                    ep_evt[NT - 2] = e_ln
                    ep_evt[NT - 1] = e_ln
                self.barrier()

    def ffn(self, src_dram, dst_dram, do_ln, tag):
        nc, T, Sc, V, Q = self.nc, self.T, self.S, self.V, self.Q
        st, ss, sv, sq, sg = self.st, self.ss, self.sv, self.sq, self.sg
        di = self.di
        with ExitStack() as p1:
            E = p1.enter_context
            w1s = [E(nc.sbuf_tensor(f"{tag}w1_{t}", [128, 4096], BF16)) for t in range(NT)]
            hstage = [E(nc.sbuf_tensor(f"{tag}hs{i}", [128, 512], BF16)) for i in range(2)]
            pph = [E(nc.psum_tensor(f"{tag}pph{i}", [128, 512], F32)) for i in range(2)]
            self.new_wt_epoch()
            for t in range(NT):
                self.lg_w(t, w1s[t][:, :], di["w1"][t * 128:(t + 1) * 128, :])
            h_free = [0, 0]
            for ch in range(8):
                cols = slice(ch * 512, ch * 512 + 512)
                for jt in range(NJ):
                    b = jt % 2
                    self.w(T, ss, h_free[b])
                    for kt in range(NT):
                        self.wt(kt)
                        ins = T.matmul(pph[b][:, :], w1s[kt][:, jt * 128:jt * 128 + 128],
                                       self.xnt[kt][:, cols], start=(kt == 0), stop=(kt == NT - 1))
                    e_h = st.inc(ins)
                    self.w(Sc, st, e_h)
                    self.w(Sc, self.sh[b], self.sh[b].n)
                    h_free[b] = ss.inc(Sc.activation(hstage[b][:, :], pph[b][:, :], AF.Gelu,
                                                     bias=self.b1_s[:, jt:jt + 1]))
                    self.w(Q, ss, h_free[b])
                    self.lq_on(self.sh[b], self.scr_h[jt * 128:jt * 128 + 128, cols],
                               hstage[b][:, :])
        self.barrier()
        with ExitStack() as p2:
            E = p2.enter_context
            w2s = [E(nc.sbuf_tensor(f"{tag}w2_{j}", [128, D], BF16)) for j in range(NJ)]
            hb = [E(nc.sbuf_tensor(f"{tag}hb{i}", [128, 4 * 512], BF16)) for i in range(4)]
            xf = E(nc.sbuf_tensor(f"{tag}xf", [128, NT * 512], F32))
            x2f = E(nc.sbuf_tensor(f"{tag}x2f", [128, NT * 512], F32))
            xb = E(nc.sbuf_tensor(f"{tag}xb", [128, NT * 512], BF16))
            xsq = E(nc.sbuf_tensor(f"{tag}xsq", [128, NT * 512], BF16))
            pyo = [E(nc.psum_tensor(f"{tag}pyo{o}", [128, 512], F32)) for o in range(NT)]
            self.new_wt_epoch()
            for j in range(NJ):
                self.lg_w(j, w2s[j][:, :], di["w2"][j * 128:(j + 1) * 128, :])
            hb_free = [0, 0, 0, 0]
            pyo_free = [0] * NT
            e_ep_prev = 0
            for ch in range(8):
                cols = slice(ch * 512, ch * 512 + 512)
                e_x = None

                def ygrp(g, gi, ots):
                    for ot in ots:
                        if g == 0:
                            self.w(T, sv, pyo_free[ot])
                        for jj in range(4):
                            jt = g * 4 + jj
                            self.wt(jt)
                            ins = T.matmul(pyo[ot][:, :],
                                           w2s[jt][:, ot * 128:ot * 128 + 128],
                                           hb[gi][:, jj * 512:jj * 512 + 512],
                                           start=(jt == 0), stop=(jt == NJ - 1))
                    return ins

                for g in range(8):
                    gi = (ch * 8 + g) % 4
                    self.w(Q, st, hb_free[gi])
                    e_hb = self.lq_on(
                        self.shh[gi],
                        hb[gi][:, :].rearrange("p (t s) -> p t s", s=512),
                        self.scr_h[g * 512:(g + 1) * 512, cols].rearrange(
                            "(t p) s -> p t s", p=128))
                    if g == 0:
                        self.w(Q, sv, e_ep_prev)
                        e_x = self.load_chunk_f32(xf, src_dram, ch * 512)
                    self.w(T, self.shh[gi], e_hb)
                    ins = ygrp(g, gi, (3, 4, 5, 6, 7))
                    if g >= 2:
                        g2 = g - 2
                        ins = ygrp(g2, (ch * 8 + g2) % 4, (0, 1, 2))
                        hb_free[(ch * 8 + g2) % 4] = st.inc(ins)
                for g2 in (6, 7):
                    ins = ygrp(g2, (ch * 8 + g2) % 4, (0, 1, 2))
                    hb_free[(ch * 8 + g2) % 4] = st.inc(ins)
                e_y = st.n
                self.w(V, st, e_y)
                self.w(V, self.sx, self.sx.n)
                self.w(V, self.sr, self.sr.n)
                V.drain()
                for ot in range(NT):
                    osl = slice(ot * 512, ot * 512 + 512)
                    pyo_free[ot] = sv.inc(
                        V.scalar_tensor_tensor(x2f[:, osl], pyo[ot][:, :],
                                               self.b2_s[:, ot:ot + 1], xf[:, osl],
                                               op0=ALU.add, op1=ALU.add))
                e_ep = sv.n
                e_ep_prev = e_ep
                self.w(Q, sv, e_ep)
                self.lq_on(self.sr,
                           dst_dram[:, cols].rearrange("(t p) s -> p t s", p=128),
                           x2f[:, :].rearrange("p (t s) -> p t s", s=512))
                if do_ln:
                    self.w(V, sv, e_ep)
                    self.w(Sc, sv, e_ep)
                    e_ln = self.emit_ln(x2f, xb, xsq, self.xnt, ch * 512,
                                        pyo[0], pyo[1], pyo[2],
                                        xb_free=pyo_free[0])
                    pyo_free[0] = pyo_free[1] = pyo_free[2] = e_ln
        self.barrier()

    def attn_b(self):
        nc, T, Sc, V, Q = self.nc, self.T, self.S, self.V, self.Q
        st, ss, sv, sq, sg = self.st, self.ss, self.sv, self.sq, self.sg
        di = self.di
        with ExitStack() as outer:
            EO = outer.enter_context
            zkt0 = EO(nc.sbuf_tensor("zkt0", [128, S], BF16))
            zkt1 = EO(nc.sbuf_tensor("zkt1", [64, S], BF16))
            xvt0 = EO(nc.sbuf_tensor("xvt0", [128, S], BF16))
            xvt1 = EO(nc.sbuf_tensor("xvt1", [64, S], BF16))
            cosr = EO(nc.sbuf_tensor("cosr", [128, S], BF16))
            sinr = EO(nc.sbuf_tensor("sinr", [128, S], BF16))
            b16_0 = EO(nc.sbuf_tensor("b16_0", [128, D], BF16))
            b16_1 = EO(nc.sbuf_tensor("b16_1", [64, D], BF16))
            vdec0 = EO(nc.sbuf_tensor("vdec0", [128, D], BF16))
            vdec1 = EO(nc.sbuf_tensor("vdec1", [64, D], BF16))
            with ExitStack() as bb:
                E = bb.enter_context
                kp0 = E(nc.sbuf_tensor("kp0", [128, D], BF16))
                kp1 = E(nc.sbuf_tensor("kp1", [64, D], BF16))
                kn0 = E(nc.sbuf_tensor("kn0", [128, D], BF16))
                kn1 = E(nc.sbuf_tensor("kn1", [64, D], BF16))
                wkb = [E(nc.sbuf_tensor(f"wkb{t}", [128, VR], BF16)) for t in range(NT)]
                wvb = [E(nc.sbuf_tensor(f"wvb{t}", [128, VR], BF16)) for t in range(NT)]
                wqp = [E(nc.sbuf_tensor(f"wqp{t}", [128, D], BF16)) for t in range(NT)]
                p8 = [E(nc.psum_tensor(f"p8_{i}", [128, 512], F32)) for i in range(8)]
                self.new_wt_epoch()
                for t in range(NT):
                    self.lg_w(t, wkb[t][:, :], di["Wkb"][t * 128:(t + 1) * 128, :])
                    self.lg_w(8 + t, wvb[t][:, :], di["Wvb"][t * 128:(t + 1) * 128, :])
                for t in range(NT):
                    self.lg_w(16 + t, wqp[t][:, :], di["Wqb_perm"][t * 128:(t + 1) * 128, :])
                self.lq(cosr[:, :], di["COSR"][:, :])
                self.lq(sinr[:, :], di["SINR"][:, :])
                self.lg(vdec0[:, :], di["Vdec"][0:128, :])
                self.lg(vdec1[:, :], di["Vdec"][128:192, :])
                self.lg(kp0[:, :], di["KDpos"][0:128, :])
                self.lg(kp1[:, :], di["KDpos"][128:192, :])
                self.lg(kn0[:, :], di["KDneg"][0:128, :])
                e_kd = self.lg(kn1[:, :], di["KDneg"][128:192, :])
                self.w(V, sg, e_kd)
                V.tensor_tensor(b16_0[:, :], kp0[:, :], kn0[:, :], ALU.subtract)
                sv.inc(V.tensor_tensor(b16_1[:, :], kp1[:, :], kn1[:, :], ALU.subtract))
                for ch in range(8):
                    cols = slice(ch * 512, ch * 512 + 512)
                    self.w(T, sv, sv.n)
                    for kt in range(NT):
                        self.wt(kt)
                        T.matmul(p8[0][:, :], wkb[kt][:, 0:128], self.xnt[kt][:, cols],
                                 start=(kt == 0), stop=(kt == NT - 1))
                    for kt in range(NT):
                        T.matmul(p8[1][0:64, :], wkb[kt][:, 128:192], self.xnt[kt][:, cols],
                                 start=(kt == 0), stop=(kt == NT - 1))
                    for kt in range(NT):
                        self.wt(8 + kt)
                        T.matmul(p8[2][:, :], wvb[kt][:, 0:128], self.xnt[kt][:, cols],
                                 start=(kt == 0), stop=(kt == NT - 1))
                    for kt in range(NT):
                        ins = T.matmul(p8[3][0:64, :], wvb[kt][:, 128:192], self.xnt[kt][:, cols],
                                       start=(kt == 0), stop=(kt == NT - 1))
                    e_zx = st.inc(ins)
                    self.w(V, st, e_zx)
                    V.tensor_copy(zkt0[:, cols], p8[0][:, :])
                    V.tensor_copy(zkt1[:, cols], p8[1][0:64, :])
                    V.tensor_copy(xvt0[:, cols], p8[2][:, :])
                    e_zxe = sv.inc(V.tensor_copy(xvt1[:, cols], p8[3][0:64, :]))
                    self.w(T, sv, e_zxe)
                    for ot in range(NT):
                        for kt in range(NT):
                            self.wt(16 + kt)
                            ins = T.matmul(p8[ot][:, :],
                                           wqp[kt][:, ot * 128:ot * 128 + 128],
                                           self.xnt[kt][:, cols], start=(kt == 0),
                                           stop=(kt == NT - 1))
                    e_qmm = st.inc(ins)
                    self.w(V, st, e_qmm)
                    for ot in range(NT):
                        ins = V.tensor_copy(self.xnt[ot][:, cols], p8[ot][:, :])
                    sv.inc(ins)
                self.barrier()

            with ExitStack() as rp:
                E = rp.enter_context
                tA = E(nc.sbuf_tensor("ropeA", [128, S], BF16))
                tB = E(nc.sbuf_tensor("ropeB", [128, S], BF16))

                def rope_q(t):
                    self.w(V, sq, sq.n)
                    V.drain()
                    V.tensor_copy(tB[0:32, :], self.xnt[t][32:64, :])
                    V.tensor_copy(tB[32:64, :], self.xnt[t][0:32, :])
                    V.tensor_copy(tB[64:96, :], self.xnt[t][96:128, :])
                    V.tensor_copy(tB[96:128, :], self.xnt[t][64:96, :])
                    V.drain()
                    V.tensor_tensor(tB[:, :], tB[:, :], sinr[:, :], ALU.mult)
                    V.tensor_tensor(self.xnt[t][:, :], self.xnt[t][:, :],
                                    cosr[:, :], ALU.mult)
                    V.drain()
                    return sv.inc(V.tensor_tensor(self.xnt[t][:, :],
                                                  self.xnt[t][:, :], tB[:, :],
                                                  ALU.add))

                kpre = tA
                vpair = [E(nc.sbuf_tensor(f"vpair{i}", [128, 32 * 130], BF16))
                         for i in range(2)]
                expb = [E(nc.sbuf_tensor(f"expb{i}", [128, 1024], BF16)) for i in range(2)]
                ybuf = [E(nc.sbuf_tensor(f"ybuf{i}", [64, 512], BF16)) for i in range(2)]
                r_d01 = [E(nc.sbuf_tensor(f"r_d01_{i}", [1, 1024], BF16)) for i in range(2)]
                kpair = [E(nc.sbuf_tensor(f"kpair{i}", [128, S], BF16)) for i in range(2)]
                trim = E(nc.sbuf_tensor("trim", [128, 128], BF16))
                psc = [E(nc.psum_tensor(f"psc{i}", [128, 1024], F32)) for i in range(2)]
                pys = [[E(nc.psum_tensor(f"py{s_}{h_}", [128, 512], F32)) for h_ in range(2)]
                       for s_ in range(2)]

                vap = [vpair[i][:, :].rearrange("p (t c) -> p t c", c=130)
                       for i in range(2)]
                for i in range(2):
                    V.memset(vap[i][:, :, 64:65], 1.0)
                    sv.inc(V.memset(vap[i][:, :, 129:130], 1.0))

                exp_free = [0, 0]
                expb_cons = [0, 0]
                py_evac = [0, 0]
                py_recip = [0, 0]
                pw_evt = [0, 0]
                prod_bank_evt = [[0, 0], [0, 0]]
                kpair_done = [0, 0]
                vpair_done = [0, 0]
                rd_i = [0]
                nc_i = [0]
                pend_y = None
                self.lq(trim[:, :], di["TRIM"][:, :])
                self.w(T, sg, sg.n)
                self.w(V, sq, sq.n)
                rope_evt = [0] * (NT + 1)
                rope_evt[0] = rope_q(0)

                def prod_kpre2(pp, kc0, pset):
                    for i, kc in enumerate((kc0, kc0 + 1)):
                        cols = slice(kc * 512, kc * 512 + 512)
                        bank = pys[pset][i]
                        self.w(T, sv, py_evac[pset])
                        self.w(T, sv, prod_bank_evt[pset][i])
                        self.w(T, ss, py_recip[pset])
                        T.matmul(bank[:, :], b16_0[:, pp * 128:pp * 128 + 128],
                                 zkt0[:, cols], start=True, stop=False)
                        e_kp = st.inc(T.matmul(bank[:, :],
                                               b16_1[0:64, pp * 128:pp * 128 + 128],
                                               zkt1[0:64, cols],
                                               start=False, stop=True))
                        self.w(V, st, e_kp)
                        ev = sv.inc(V.tensor_copy(kpre[:, cols], bank[:, :]))
                        prod_bank_evt[pset][i] = ev
                        pw_evt[pset] = ev

                def prod_krope(pp):
                    kb = kpair[pp % 2]
                    V.drain()
                    V.tensor_copy(tB[0:32, :], kpre[32:64, :])
                    V.tensor_copy(tB[32:64, :], kpre[0:32, :])
                    V.tensor_copy(tB[64:96, :], kpre[96:128, :])
                    V.tensor_copy(tB[96:128, :], kpre[64:96, :])
                    V.drain()
                    V.tensor_tensor(kb[:, :], kpre[:, :], cosr[:, :], ALU.mult)
                    V.tensor_tensor(tB[:, :], tB[:, :], sinr[:, :], ALU.mult)
                    V.drain()
                    kpair_done[pp % 2] = sv.inc(
                        V.tensor_tensor(kb[:, :], kb[:, :], tB[:, :], ALU.add))

                def prod_vpair(pp, g0, ng, pset):
                    vb = vap[pp % 2]
                    for gi in range(ng):
                        g = g0 + gi
                        bank = pys[pset][gi % 2]
                        self.w(T, sv, py_evac[pset])
                        self.w(T, sv, prod_bank_evt[pset][gi % 2])
                        self.w(T, ss, py_recip[pset])
                        for s4 in range(4):
                            stt = g * 4 + s4
                            ssl = slice(stt * 128, stt * 128 + 128)
                            csl = slice(s4 * 128, s4 * 128 + 128)
                            T.matmul(bank[:, csl], xvt0[:, ssl],
                                     vdec0[:, pp * 128:pp * 128 + 128],
                                     start=True, stop=False)
                            ins = T.matmul(bank[:, csl], xvt1[0:64, ssl],
                                           vdec1[0:64, pp * 128:pp * 128 + 128],
                                           start=False, stop=True)
                        e_vp = st.inc(ins)
                        self.w(V, st, e_vp)
                        srcb = bank[:, :].rearrange("p (g c) -> p g c", c=128)
                        V.tensor_copy(vb[:, g * 4:g * 4 + 4, 0:64],
                                      srcb[:, :, 0:64])
                        ev = sv.inc(V.tensor_copy(vb[:, g * 4:g * 4 + 4, 65:129],
                                                  srcb[:, :, 64:128]))
                        prod_bank_evt[pset][gi % 2] = ev
                        pw_evt[pset] = ev
                    vpair_done[pp % 2] = pw_evt[pset]
                for p in range(8):
                    pp_next = p + 1 if p + 1 < 8 else None
                    if p == 0:
                        for c0 in range(4):
                            prod_kpre2(0, 2 * c0, (c0 + 1) % 2)
                        prod_krope(0)
                        prod_vpair(0, 0, 4, 1)
                        prod_vpair(0, 4, 4, 0)
                    self.w(T, sv, vpair_done[p % 2])
                    self.w(T, sv, kpair_done[p % 2])
                    self.w(T, sv, rope_evt[p])
                    def emit_tail(pp, cc, pyset, e_ylast):
                        py0, py1 = pys[pyset]
                        rb = rd_i[0]
                        rd_i[0] ^= 1
                        self.w(Sc, st, e_ylast)
                        self.w(Sc, self.sh[rb], self.sh[rb].n)
                        Sc.activation(self.r_rstd[:, :], py0[64:65, :], AF.Ln)
                        Sc.activation(self.r_var[:, :], py1[64:65, :], AF.Ln)
                        Sc.drain()
                        Sc.activation(r_d01[rb][:, 0:512], self.r_rstd[:, :], AF.Exp,
                                      scale=-1.0)
                        py_recip[pyset] = ss.inc(
                            Sc.activation(r_d01[rb][:, 512:1024], self.r_var[:, :],
                                          AF.Exp, scale=-1.0))
                        self.w(V, st, e_ylast)
                        self.w(V, self.sh[0], self.sh[0].n)
                        self.w(V, self.sh[1], self.sh[1].n)
                        V.drain()
                        V.tensor_copy(ybuf[0][:, :], py0[0:64, :])
                        py_evac[pyset] = sv.inc(V.tensor_copy(ybuf[1][:, :], py1[0:64, :]))
                        self.w(Q, sv, py_evac[pyset])
                        self.w(Q, ss, py_recip[pyset])
                        h0 = (pp * 2) * 64
                        h1 = (pp * 2 + 1) * 64
                        ccs = slice(cc * 512, cc * 512 + 512)
                        self.lq_on(self.sh[0], self.yscr[h0:h0 + 64, ccs], ybuf[0][:, :])
                        self.lq_on(self.sh[1], self.yscr[h1:h1 + 64, ccs], ybuf[1][:, :])
                        self.lq_on(self.sh[rb], self.drec[pp * 2:pp * 2 + 1, ccs],
                                   r_d01[rb][:, 0:512])
                        self.lq_on(self.sh[rb], self.drec[pp * 2 + 1:pp * 2 + 2, ccs],
                                   r_d01[rb][:, 512:1024])

                    def emit_y(pc, pt, pcol0, pbank, pexp, pyset, pmask):
                        plast = 4 * pc + 3
                        py0, py1 = pys[pyset]
                        self.w(T, ss, pexp)
                        if pmask is not None:
                            self.w(T, sv, pmask)
                        if pt == 0:
                            self.w(T, sv, py_evac[pyset])
                            self.w(T, sv, pw_evt[pyset])
                            self.w(T, ss, py_recip[pyset])
                        T.matmul(py0[0:65, pcol0:512],
                                 vpair[p % 2][:, pt * 130:pt * 130 + 65],
                                 expb[pbank][:, pcol0:512],
                                 start=(pt == 0), stop=(pt == plast))
                        ins = T.matmul(py1[0:65, pcol0:512],
                                       vpair[p % 2][:, pt * 130 + 65:pt * 130 + 130],
                                       expb[pbank][:, 512 + pcol0:1024],
                                       start=(pt == 0), stop=(pt == plast))
                        expb_cons[pbank] = st.inc(ins)

                    tiles = [(c, t) for c in range(8) for t in range(4 * c + 4)]
                    for i_f, (c, t) in enumerate(tiles):
                        m = t - 4 * c
                        col0 = 128 * m if m >= 0 else 0
                        bank = i_f % 2
                        tsl = slice(t * 128, t * 128 + 128)
                        self.w(T, ss, exp_free[bank])
                        T.matmul(psc[bank][:, col0:512],
                                 kpair[p % 2][0:64, tsl],
                                 self.xnt[p][0:64, c * 512 + col0:(c + 1) * 512],
                                 start=True, stop=(m < 0))
                        if m >= 0:
                            T.matmul(psc[bank][:, col0:col0 + 128], self.lmat[:, :],
                                     self.ident[:, :], start=False, stop=True)
                        ins = T.matmul(psc[bank][:, 512 + col0:1024],
                                       kpair[p % 2][64:128, tsl],
                                       self.xnt[p][64:128, c * 512 + col0:(c + 1) * 512],
                                       start=True, stop=(m < 0))
                        if m >= 0:
                            ins = T.matmul(psc[bank][:, 512 + col0:512 + col0 + 128],
                                           self.lmat[:, :], self.ident[:, :],
                                           start=False, stop=True)
                        e_sc = st.inc(ins)
                        self.w(Sc, st, e_sc)
                        self.w(Sc, st, expb_cons[bank])
                        if m < 0:
                            exp_free[bank] = ss.inc(Sc.activation(
                                expb[bank][:, :], psc[bank][:, :], AF.Exp, scale=0.125))
                        else:
                            Sc.activation(expb[bank][:, col0:512],
                                          psc[bank][:, col0:512], AF.Exp, scale=0.125)
                            exp_free[bank] = ss.inc(Sc.activation(
                                expb[bank][:, 512 + col0:1024],
                                psc[bank][:, 512 + col0:1024], AF.Exp, scale=0.125))
                        exp_mask = None
                        if pend_y is not None:
                            emit_y(*pend_y)
                            if pend_y[1] == 4 * pend_y[0] + 3:
                                emit_tail(p, pend_y[0], pend_y[5], st.n)
                        pend_y = (c, t, col0, bank, exp_free[bank], nc_i[0] % 2, exp_mask)
                        if pp_next is not None:
                            if t == 1 and c <= 3:
                                prod_kpre2(pp_next, 2 * c, (c + 1) % 2)
                            elif t == 2 and c == 4:
                                rope_evt[pp_next] = rope_q(pp_next)
                            elif t == 2 and c == 5:
                                prod_krope(pp_next)
                            elif t == 1 and c == 6:
                                prod_vpair(pp_next, 0, 4, (c + 1) % 2)
                            elif t == 1 and c == 7:
                                prod_vpair(pp_next, 4, 4, (c + 1) % 2)
                        if t == 4 * c + 3:
                            nc_i[0] += 1
                    emit_y(*pend_y)
                    emit_tail(p, pend_y[0], pend_y[5], st.n)
                    pend_y = None
                self.barrier()


        with ExitStack() as p3b:
            E = p3b.enter_context
            wpr = [E(nc.sbuf_tensor(f"wpr{t}", [128, D], BF16)) for t in range(NT)]
            sel2 = E(nc.sbuf_tensor("sel2", [2, 128], BF16))
            drecb = [E(nc.sbuf_tensor(f"drecb{i}", [2, 8 * 512], BF16)) for i in range(2)]
            ybT = [E(nc.sbuf_tensor(f"ybT{i}", [128, 8 * 512], BF16)) for i in range(2)]
            xf = E(nc.sbuf_tensor("p3_xf", [128, NT * 512], F32))
            x3f = E(nc.sbuf_tensor("p3_x3f", [128, NT * 512], F32))
            xb = E(nc.sbuf_tensor("p3_xb", [128, NT * 512], BF16))
            xsq = E(nc.sbuf_tensor("p3_xsq", [128, NT * 512], BF16))
            pr = [E(nc.psum_tensor(f"pr{i}", [128, 512], F32)) for i in range(2)]
            pb = [E(nc.psum_tensor(f"pb{i}", [128, 512], F32)) for i in range(2)]
            pbc = [E(nc.psum_tensor(f"pbc{i}", [128, 512], F32)) for i in range(2)]
            pst3 = E(nc.psum_tensor("pst3", [33, 512], F32))
            self.lq(sel2[:, :], di["SEL2"][:, :])
            self.new_wt_epoch()
            for t in range(NT):
                self.lg_w(t, wpr[t][:, :], di["Wproj"][t * 128:(t + 1) * 128, :])
            self.w(T, sq, sq.n)
            bcast_free = [0, 0]
            proj_done = [0, 0]
            ybn_evt = [0, 0]
            pr_free = [0, 0]
            ln3_prev = [0]
            ep_prev = [0]

            def stageA(ch):
                b = ch % 2
                cols = slice(ch * 512, ch * 512 + 512)
                self.w(Q, st, proj_done[b])
                self.lq_on(self.sh[b], ybT[b][:, :].rearrange("p (t s) -> p t s", s=512),
                           self.yscr[:, cols].rearrange("(t p) s -> p t s", p=128))
                self.lq_on(self.sh[b], drecb[b][:, :].rearrange("p (t s) -> p t s", s=512),
                           self.drec[:, cols].rearrange("(t two) s -> two t s", two=2))
                first = True
                for kt in range(NT):
                    bb = kt % 2
                    self.w(T, sv, bcast_free[bb])
                    if first:
                        self.w(T, self.sh[b], self.sh[b].n)
                        first = False
                    e_bc = st.inc(T.matmul(pbc[bb][:, :], sel2[:, :],
                                           drecb[b][:, kt * 512:kt * 512 + 512],
                                           start=True, stop=True))
                    self.w(V, st, e_bc)
                    if kt == 0:
                        self.w(V, st, proj_done[b])
                        V.drain()
                    bcast_free[bb] = sv.inc(V.tensor_tensor(
                        ybT[b][:, kt * 512:kt * 512 + 512],
                        ybT[b][:, kt * 512:kt * 512 + 512], pbc[bb][:, :], ALU.mult))
                ybn_evt[b] = sv.n

            def stageB(ch):
                b = ch % 2
                nb = (ch + 1) % 2
                pre = ch + 1 < 8
                c0 = ch * 512
                cols = slice(c0, c0 + 512)
                self.w(Q, sv, ep_prev[0])
                e_x = self.load_chunk_f32(xf, self.scr2, c0)
                if pre:
                    ncols = slice(c0 + 512, c0 + 1024)
                    self.w(Q, st, proj_done[nb])
                    self.lq_on(self.sh[nb],
                               ybT[nb][:, :].rearrange("p (t s) -> p t s", s=512),
                               self.yscr[:, ncols].rearrange("(t p) s -> p t s", p=128))
                    self.lq_on(self.sh[nb],
                               drecb[nb][:, :].rearrange("p (t s) -> p t s", s=512),
                               self.drec[:, ncols].rearrange("(t two) s -> two t s",
                                                             two=2))
                self.w(T, sv, ybn_evt[b])
                for ot in range(NT):
                    pb_ = ot % 2
                    self.w(T, sv, pr_free[pb_])
                    for kt in range(NT):
                        self.wt(kt)
                        ins = T.matmul(pr[pb_][:, :], wpr[kt][:, ot * 128:ot * 128 + 128],
                                       ybT[b][:, kt * 512:kt * 512 + 512],
                                       start=(kt == 0), stop=(kt == NT - 1))
                    e_o = st.inc(ins)
                    if pre:
                        self.w(T, sv, bcast_free[pb_])
                        if ot == 0:
                            self.w(T, self.sh[nb], self.sh[nb].n)
                        e_bc = st.inc(T.matmul(pbc[pb_][:, :], sel2[:, :],
                                               drecb[nb][:, ot * 512:ot * 512 + 512],
                                               start=True, stop=True))
                    self.w(V, st, e_o)
                    self.w(V, self.sx, self.sx.n)
                    self.w(V, self.sr, self.sr.n)
                    if ot == 0:
                        V.drain()
                    osl = slice(ot * 512, ot * 512 + 512)
                    pr_free[pb_] = sv.inc(V.tensor_tensor(x3f[:, osl], pr[pb_][:, :],
                                                          xf[:, osl], ALU.add))
                    if pre:
                        self.w(V, st, e_bc)
                        bcast_free[pb_] = sv.inc(V.tensor_tensor(
                            ybT[nb][:, ot * 512:ot * 512 + 512],
                            ybT[nb][:, ot * 512:ot * 512 + 512],
                            pbc[pb_][:, :], ALU.mult))
                if pre:
                    ybn_evt[nb] = sv.n
                proj_done[b] = st.n
                e_ep = sv.n
                ep_prev[0] = e_ep
                self.w(Q, sv, e_ep)
                self.lq_on(self.sr,
                           self.scr3[:, cols].rearrange("(t p) s -> p t s", p=128),
                           x3f[:, :].rearrange("p (t s) -> p t s", s=512))
                self.w(V, sv, e_ep)
                self.w(Sc, sv, e_ep)
                ln3_prev[0] = self.emit_ln(x3f, xb, xsq, self.xnt, c0, pst3,
                                           pb[0], pb[1], xb_free=ln3_prev[0])

            stageA(0)
            for ch in range(8):
                stageB(ch)
        self.barrier()

    def finish(self):
        self.Q.wait_ge(self.sq.h, self.sq.n)
        self.Q.wait_ge(self.sr.h, self.sr.n)
        self.Q.wait_ge(self.sh[0].h, self.sh[0].n)
        self.Q.wait_ge(self.sh[1].h, self.sh[1].n)
        self.G.wait_ge(self.sg.h, self.sg.n)


def build_program():
    nc = bass.Bass("TRN2", target_bir_lowering=False)
    with ExitStack() as stack:
        mk = MK(nc, stack)
        mk.phase0()
        if _PHASES >= 1:
            mk.attn_a()
        if _PHASES >= 2:
            mk.ffn(mk.scr1, mk.scr2, do_ln=True, tag="f1")
        if _PHASES >= 3:
            mk.attn_b()
        if _PHASES >= 4:
            mk.ffn(mk.scr3, mk.d_out, do_ln=False, tag="f2")
        else:
            src = {0: mk.dxT, 1: mk.scr1, 2: mk.scr2, 3: mk.scr3}[min(_PHASES, 3)]
            with ExitStack() as dbg:
                buf = dbg.enter_context(nc.sbuf_tensor("dbgbuf", [128, 4096], F32))
                for ch in range(8):
                    cols = slice(ch * 512, ch * 512 + 512)
                    mk.w(mk.Q, mk.sq, mk.sq.n)
                    mk.lq(buf[:, :].rearrange("p (t s) -> p t s", s=512),
                          src[:, cols].rearrange("(t p) s -> p t s", p=128))
                    mk.w(mk.Q, mk.sq, mk.sq.n)
                    mk.lq(mk.d_out[:, cols].rearrange("(t p) s -> p t s", p=128),
                          buf[:, :].rearrange("p (t s) -> p t s", s=512))
        mk.finish()
    return nc



def _host_prep(inputs):
    g = lambda n: np.ascontiguousarray(np.asarray(inputs[n], dtype=np.float32))
    x = g("L0_attn_a_x")
    xT = np.ascontiguousarray(x.transpose(0, 2, 1))

    shared = {}
    shared["WqaT"] = np.ascontiguousarray(
        np.concatenate([g("L0_attn_a_Wq").T, g("L0_attn_a_bq")[None, :]], axis=0))
    shared["bqa"] = np.ascontiguousarray(g("L0_attn_a_bq").reshape(NT, 128).T)
    shared["WkaT"] = np.ascontiguousarray(
        np.concatenate([g("L0_attn_a_Wk").T, g("L0_attn_a_bk")[None, :]], axis=0))
    shared["WvaT"] = np.ascontiguousarray(
        np.concatenate([g("L0_attn_a_Wv").T, g("L0_attn_a_bv")[None, :]], axis=0))
    shared["WoaT"] = np.ascontiguousarray(g("L0_attn_a_Wo").T)
    shared["boa"] = np.ascontiguousarray(g("L0_attn_a_bo")[None, :])
    shared["w1"] = g("L0_ffn_w1")
    shared["b1c"] = np.ascontiguousarray(g("L0_ffn_b1").reshape(NJ, 128).T)
    shared["w2"] = g("L0_ffn_w2")
    shared["b2c"] = np.ascontiguousarray(g("L0_ffn_b2").reshape(NT, 128).T)

    perm = np.zeros(D, dtype=np.int64)
    for h in range(H):
        base = (h // 2) * 128 + (h % 2) * 64
        for j in range(FF):
            perm[base + j] = h * HD + 2 * j
            perm[base + 32 + j] = h * HD + 2 * j + 1
    shared["Wqb_perm"] = np.ascontiguousarray(g("L0_attn_b_Wq")[:, perm])
    shared["Wkb"] = g("L0_attn_b_Wk")
    shared["Wvb"] = g("L0_attn_b_Wv")
    kd = g("L0_attn_b_key_decoder")
    br_, bi_ = kd[..., 0], kd[..., 1]
    KDpos = np.zeros((VR, D), np.float32)
    KDneg = np.zeros((VR, D), np.float32)
    for f in range(FF):
        for r in range(KR):
            for h in range(H):
                c1 = (h // 2) * 128 + (h % 2) * 64 + f
                c2 = (h // 2) * 128 + (h % 2) * 64 + 32 + f
                KDpos[f * 6 + r * 2 + 0, c1] = br_[f, h, r]
                KDpos[f * 6 + r * 2 + 0, c2] = bi_[f, h, r]
                KDpos[f * 6 + r * 2 + 1, c2] = br_[f, h, r]
                KDneg[f * 6 + r * 2 + 1, c1] = bi_[f, h, r]
    shared["KDpos"] = KDpos
    shared["KDneg"] = KDneg
    shared["Vdec"] = np.ascontiguousarray(
        g("L0_attn_b_value_decoder").transpose(1, 0, 2).reshape(VR, D))
    shared["Wproj"] = g("L0_attn_b_Wproj")

    half = HD // 2
    freq = 1.0 / 10000.0 ** (np.arange(half, dtype=np.float32) / half)
    freqs = np.outer(np.arange(S, dtype=np.float32), freq)
    cos = np.cos(freqs).astype(np.float32)
    sin = np.sin(freqs).astype(np.float32)
    p_idx = np.arange(128)
    jmod = p_idx % 32
    sign = np.where((p_idx % 64) < 32, -1.0, 1.0).astype(np.float32)
    shared["COSR"] = np.ascontiguousarray(cos.T[jmod]).astype(BFNP)
    shared["SINR"] = np.ascontiguousarray(sin.T[jmod] * sign[:, None]).astype(BFNP)
    TRI = np.triu(np.ones((128, 128), np.float32))
    shared["LMAT"] = np.ascontiguousarray(((-1e9) * (1.0 - TRI)).astype(BFNP).T)
    shared["IDENT"] = np.eye(128, dtype=BFNP)
    sel2 = np.zeros((2, 128), dtype=BFNP)
    sel2[0, 0:64] = 1
    sel2[1, 64:128] = 1
    shared["SEL2"] = sel2
    shared["TRIM"] = TRI.astype(BFNP)
    return xT, shared


def kernel(**inputs):
    xT, shared = _host_prep(inputs)
    nc = build_program()
    in_maps = [dict(shared, xT=np.ascontiguousarray(xT[c])) for c in range(NC)]
    res = run_bass_kernel_spmd(nc, in_maps, core_ids=list(range(NC)))
    out = np.stack([np.asarray(r["outT"], np.float32).T for r in res.results], axis=0)
    return np.ascontiguousarray(out)
